# version 05
# baseline (speedup 1.0000x reference)
import bisect
import numpy as np
import ml_dtypes
from contextlib import ExitStack
from collections import defaultdict

import concourse.bass as bass
import concourse.mybir as mybir
from concourse.bass_utils import run_bass_kernel_spmd

F32 = mybir.dt.float32
BF16 = mybir.dt.bfloat16
AF = mybir.ActivationFunctionType
ALU = mybir.AluOpType
AX = mybir.AxisListType

B = 8
L = 4096
DM = 2048
DIN = 8512
DI = 4096
DCC = 4352
NH = 64
HD = 64
NST = 128
CH = 128
NCH = 32
NE = 16
IE = 1408
EPS = 1e-5

TT1 = 1024
NT1 = L // TT1
TT5 = 512
NT5 = L // TT5

INC = {"sp": 16, "gd": 16, "ad": 16}
PHYS = {"pe": "tensor", "act": "scalar", "dve": "vector", "pool": "gpsimd",
        "sp": "sync", "gd": "gpsimd", "ad": "scalar"}


class Sch:

    def __init__(self):
        self.ops = []

    def add(self, key, emit, reads=(), writes=(), tag=""):
        assert key in PHYS
        assert not isinstance(reads, str) and not isinstance(writes, str)
        for x in tuple(reads) + tuple(writes):
            assert isinstance(x, str) and len(x) > 1, (x, tag)
        self.ops.append([key, emit, tuple(reads), tuple(writes), tag])

    def barrier(self):
        self.ops.append(["BARRIER", None, (), (), "barrier"])

    SYNCKEYS = ("dve", "act", "pool")
    STRICT = True

    def analyze(self):
        seq = defaultdict(int)
        lastw = {}
        readers = defaultdict(list)
        clocks = defaultdict(dict)
        selfw = defaultdict(lambda: -1)
        snap = {}
        marks = defaultdict(set)
        self.deps = []
        bar = None
        for op in self.ops:
            key = op[0]
            if key == "BARRIER":
                bar = {k: seq[k] - 1 for k in seq if seq[k] > 0}
                self.deps.append(None)
                continue
            s = seq[key]
            seq[key] += 1
            clk = clocks[key]
            need = {}
            selfsync = key in self.SYNCKEYS

            def nd(k2, s2):
                if k2 == key and not selfsync:
                    return
                if need.get(k2, -1) < s2:
                    need[k2] = s2
            if self.STRICT:
                for k2 in list(seq):
                    s2 = seq[k2] - 1 - (1 if k2 == key else 0)
                    if s2 >= 0 and k2 != key:
                        nd(k2, s2)
                    elif s2 >= 0 and key in self.SYNCKEYS:
                        nd(k2, s2)
            if bar is not None:
                for k2, s2 in bar.items():
                    nd(k2, s2)
            for r in op[2]:
                lw = lastw.get(r)
                if lw:
                    nd(*lw)
            for w in op[3]:
                lw = lastw.get(w)
                if lw:
                    nd(*lw)
                for rd in readers.get(w, ()):
                    nd(*rd)
            waits = {}
            for k2, s2 in need.items():
                if k2 == key:
                    if selfw[key] < s2:
                        waits[k2] = s2
                        selfw[key] = s2
                else:
                    if clk.get(k2, -1) < s2:
                        waits[k2] = s2
            for k2, s2 in waits.items():
                sn = snap[(k2, s2)]
                for k3, s3 in sn.items():
                    if clk.get(k3, -1) < s3:
                        clk[k3] = s3
                marks[k2].add(s2)
            clk[key] = s
            snap[(key, s)] = dict(clk)
            self.deps.append(waits)
            for r in op[2]:
                readers[r].append((key, s))
            for w in op[3]:
                lastw[w] = (key, s)
                readers[w] = [(key, s)]
        for k in INC:
            if seq.get(k, 0):
                marks[k].update(range(seq[k]))
        self.marks = {k: sorted(v) for k, v in marks.items()}
        self.nseq = dict(seq)

    def emit(self, nc, block, sems):
        def val(k, s):
            m = self.marks.get(k, [])
            return bisect.bisect_right(m, s) * INC.get(k, 1)
        i = 0
        n = len(self.ops)
        seq = defaultdict(int)
        while i < n:
            if self.ops[i][0] == "BARRIER":
                i += 1
                continue
            phys = PHYS[self.ops[i][0]]
            j = i
            while j < n and self.ops[j][0] != "BARRIER" \
                    and PHYS[self.ops[j][0]] == phys:
                j += 1
            lo, hi = i, j

            def body(eng, lo=lo, hi=hi):
                for t in range(lo, hi):
                    key, emit, _, _, tag = self.ops[t]
                    s = seq[key]
                    seq[key] += 1
                    for k2, s2 in sorted(self.deps[t].items()):
                        eng.wait_ge(sems[k2], val(k2, s2))
                    instr = emit(eng)
                    if s in self.marks.get(key, []):
                        assert instr is not None, (key, s, tag)
                        instr.then_inc(sems[key], INC.get(key, 1))
            getattr(block, phys)(body)
            i = j
        def final(eng):
            for k in self.marks:
                m = self.marks[k]
                if m:
                    eng.wait_ge(sems[k], len(m) * INC.get(k, 1))
        block.sync(final)


def ceil_div(a, b):
    return (a + b - 1) // b


def build(debug=()):
    nc = bass.Bass("TRN2", target_bir_lowering=False)
    dt = nc.dram_tensor

    u = dt("u", [L, DM], F32, kind="ExternalInput").ap()
    winf = dt("winf", [DM, DIN], F32, kind="ExternalInput").ap()
    woutf = dt("woutf", [DI, DM], F32, kind="ExternalInput").ap()
    gpTf = dt("gpTf", [NE, DM, IE], F32, kind="ExternalInput").ap()
    upTf = dt("upTf", [NE, DM, IE], F32, kind="ExternalInput").ap()
    dpTf = dt("dpTf", [NE, IE, DM], F32, kind="ExternalInput").ap()
    shgTf = dt("shgTf", [DM, 2 * IE], F32, kind="ExternalInput").ap()
    shuTf = dt("shuTf", [DM, 2 * IE], F32, kind="ExternalInput").ap()
    shdTf = dt("shdTf", [2 * IE, DM], F32, kind="ExternalInput").ap()
    gwTf = dt("gwTf", [DM, NE], F32, kind="ExternalInput").ap()
    convw = dt("convw", [128, 136], F32, kind="ExternalInput").ap()
    convb = dt("convb", [128, 34], F32, kind="ExternalInput").ap()
    dtbbc = dt("dtbbc", [128, NH], F32, kind="ExternalInput").ap()
    alogbc = dt("alogbc", [128, NH], F32, kind="ExternalInput").ap()
    dbc = dt("dbc", [128, NH], F32, kind="ExternalInput").ap()
    normwbc = dt("normwbc", [128, DI], F32, kind="ExternalInput").ap()
    ebiasbc = dt("ebiasbc", [128, NE], F32, kind="ExternalInput").ap()
    efacbc = dt("efacbc", [128, NE], F32, kind="ExternalInput").ap()
    onessq = dt("onessq", [128, 128], F32, kind="ExternalInput").ap()
    tri = dt("tri", [128, 128], F32, kind="ExternalInput").ap()
    idb = dt("idb", [128, 128], BF16, kind="ExternalInput").ap()

    win_b = dt("win_b", [DM, DIN], BF16).ap()
    wout_b = dt("wout_b", [DI, DM], BF16).ap()
    gpT_b = dt("gpT_b", [NE, DM, IE], BF16).ap()
    upT_b = dt("upT_b", [NE, DM, IE], BF16).ap()
    dpT_b = dt("dpT_b", [NE, IE, DM], BF16).ap()
    shgT_b = dt("shgT_b", [DM, 2 * IE], BF16).ap()
    shuT_b = dt("shuT_b", [DM, 2 * IE], BF16).ap()
    shdT_b = dt("shdT_b", [2 * IE, DM], BF16).ap()
    xn_s = dt("xn_s", [16, 128, L], BF16).ap()
    xbc_s = dt("xbc_s", [34, 128, L], BF16).ap()
    siluz_s = dt("siluz_s", [L, DI], BF16).ap()
    tfm_s = dt("tfm_s", [NCH, 128, DI], BF16).ap()

    out = dt("out", [L, DM], F32, kind="ExternalOutput").ap()
    dbg = {}
    if debug:
        for name, shape, dty in [
            ("d_xn", [16, 128, L], BF16), ("d_xbc", [34, 128, L], BF16),
            ("d_siluz", [L, DI], BF16), ("d_dt", [128, 2048], F32),
            ("d_tfm", [NCH, 128, DI], BF16),
            ("d_xnbf", [L, DM], BF16), ("d_lnc", [L // 128, 128, 8], F32),
            ("d_dcol", [128, 6 * NH], F32), ("d_cumcl", [128, 128], F32),
            ("d_ecol", [128, 3 * NH], F32), ("d_xtm", [128, DI], BF16),
            ("d_cbt", [128, 128], F32), ("d_mbf", [128, NH * 128], BF16),
            ("d_yoff", [128, DI], F32), ("d_y", [128, DI], F32),
            ("d_xpre", [128, 520], BF16), ("d_xcv", [128, 34, 512], BF16),
            ("d_sco", [128, 4, 16], F32), ("d_comb", [128, 4, 16], F32),
            ("d_accw", [128, 4, 2048], F32), ("d_acc0", [128, 4, 2048], F32),
            ("d_int0", [128, 11, 512], BF16), ("d_tfmv", [128, 32, 512], BF16),
            ("d_cw", [128, 136], F32), ("d_cb", [128, 34], F32),
            ("d_cacc", [128, 512], F32),
            ("d_prev", [128, DI], F32), ("d_tbf", [128, DI], BF16),
        ]:
            if name in debug:
                dbg[name] = dt(name, shape, dty, kind="ExternalOutput").ap()

    sch = Sch()
    es = ExitStack()

    def sb(name, shape, dty=F32):
        return es.enter_context(nc.sbuf_tensor(name, shape, dty))

    c_tri = sb("c_tri", [128, 128])
    c_ones = sb("c_ones", [128, 128])
    c_idb = sb("c_idb", [128, 128], BF16)
    c_dtb = sb("c_dtb", [128, NH])
    c_nA = sb("c_nA", [128, NH])
    c_dbc = sb("c_dbc", [128, NH])
    c_convw = sb("c_convw", [128, 136])
    c_convb = sb("c_convb", [128, 34])
    dt_tm = sb("dt_tm", [128, 2048])

    PS = es.enter_context(nc.psum_tensor("PS", [128, 3584], F32))
    PT = es.enter_context(nc.psum_tensor("PT", [128, 1024], BF16))
    PSB = lambda b: PS[:, 512 * b:512 * (b + 1)]

    def cast_dma(dst, src, n_pieces, res, tag):
        rows = src.shape[0]
        step = ceil_div(rows, n_pieces)
        for r0 in range(0, rows, step):
            r1 = min(rows, r0 + step)
            def e(eng, dst=dst[r0:r1], src=src[r0:r1]):
                return eng.dma_start(out=dst, in_=src)
            sch.add("gd", e, reads=(), writes=(res,), tag=tag)

    cast_dma(win_b, winf, 4, "win_b", "cast_win")
    cast_dma(wout_b, woutf, 2, "wout_b", "cast_wout")
    cast_dma(shgT_b, shgTf, 2, "shgT_b", "cast_shg")
    cast_dma(shuT_b, shuTf, 2, "shuT_b", "cast_shu")
    cast_dma(shdT_b, shdTf, 2, "shdT_b", "cast_shd")
    for e_ in range(NE):
        cast_dma(gpT_b[e_], gpTf[e_], 1, "gpT_b", "cast_gp")
        cast_dma(upT_b[e_], upTf[e_], 1, "upT_b", "cast_up")
        cast_dma(dpT_b[e_], dpTf[e_], 1, "dpT_b", "cast_dp")

    for tile, src, res in [(c_tri, tri, "c_tri"), (c_ones, onessq, "c_ones"),
                           (c_idb, idb, "c_idb"), (c_dtb, dtbbc, "c_dtb"),
                           (c_dbc, dbc, "c_dbc"), (c_convw, convw, "c_convw"),
                           (c_convb, convb, "c_convb")]:
        def e(eng, tile=tile, src=src):
            return eng.dma_start(out=tile[:], in_=src)
        sch.add("sp", e, writes=(res,), tag="const")
    c_alog = sb("c_alog", [128, NH])
    sch.add("sp", lambda eng: eng.dma_start(out=c_alog[:], in_=alogbc),
            writes=("c_alog",), tag="const")
    sch.add("act", lambda eng: eng.activation(c_nA[:], c_alog[:], AF.Exp),
            reads=("c_alog",), writes=("c_nA",), tag="negA")
    sch.add("dve", lambda eng: eng.tensor_scalar_mul(c_nA[:], c_nA[:], -1.0),
            reads=("c_nA",), writes=("c_nA",), tag="negA2")

    with ExitStack() as s1es:
        def sb1(name, shape, dty=F32):
            return s1es.enter_context(nc.sbuf_tensor(name, shape, dty))
        u_t = [sb1(f"u_t{i}", [128, DM]) for i in range(2)]
        ln_sq = sb1("ln_sq", [128, DM])
        ln_c = sb1("ln_c", [128, 8])
        xn_bf = sb1("xn_bf", [128, DM], BF16)
        xn_fm = sb1("xn_fm", [128, 16, TT1], BF16)
        win_t = [sb1(f"win_t{i}", [128, 16, 512], BF16) for i in range(2)]
        zst = [sb1(f"zst{i}", [128, 512], BF16) for i in range(2)]
        xbst = [sb1(f"xbst{i}", [128, 512], BF16) for i in range(2)]

        for tile in range(NT1):
            t0 = tile * TT1
            for g in range(8):
                sl = g % 2
                us = f"u{sl}"
                def e(eng, t0=t0, g=g, sl=sl):
                    return eng.dma_start(out=u_t[sl][:],
                                         in_=u[t0 + g * 128:t0 + (g + 1) * 128, :])
                sch.add("sp", e, writes=(us,), tag=f"uload{tile}.{g}")

                def e(eng, sl=sl):
                    return eng.tensor_reduce(ln_c[:, 0:1], u_t[sl][:], AX.X, ALU.add)
                sch.add("dve", e, reads=(us,), writes=("lnc",), tag="lnsum")

                def e(eng, sl=sl):
                    return eng.activation(ln_sq[:], u_t[sl][:], AF.Square)
                sch.add("act", e, reads=(us,), writes=("lnsq",), tag="lnsq")

                def e(eng):
                    return eng.tensor_reduce(ln_c[:, 1:2], ln_sq[:], AX.X, ALU.add)
                sch.add("dve", e, reads=("lnsq",), writes=("lnc",), tag="lnssq")

                def e(eng):
                    return eng.tensor_scalar_mul(ln_c[:, 2:3], ln_c[:, 0:1], 1.0 / DM)
                sch.add("dve", e, reads=("lnc",), writes=("lnc",), tag="mean")

                def e(eng):
                    return eng.tensor_tensor(ln_c[:, 3:4], ln_c[:, 2:3],
                                             ln_c[:, 2:3], ALU.mult)
                sch.add("dve", e, reads=("lnc",), writes=("lnc",), tag="msq")

                def e(eng):
                    return eng.scalar_tensor_tensor(ln_c[:, 4:5], ln_c[:, 1:2],
                                                    1.0 / DM, ln_c[:, 3:4],
                                                    ALU.mult, ALU.subtract)
                sch.add("dve", e, reads=("lnc",), writes=("lnc",), tag="var")

                def e(eng):
                    return eng.tensor_scalar_add(ln_c[:, 5:6], ln_c[:, 4:5], EPS)
                sch.add("dve", e, reads=("lnc",), writes=("lnc",), tag="vareps")

                def e(eng):
                    return eng.activation(ln_c[:, 6:7], ln_c[:, 5:6], AF.Ln)
                sch.add("act", e, reads=("lnc",), writes=("lnc",), tag="lnv")

                def e(eng):
                    return eng.tensor_scalar_mul(ln_c[:, 5:6], ln_c[:, 6:7], -0.5)
                sch.add("dve", e, reads=("lnc",), writes=("lnc",), tag="halfln")

                def e(eng):
                    return eng.activation(ln_c[:, 7:8], ln_c[:, 5:6], AF.Exp)
                sch.add("act", e, reads=("lnc",), writes=("lnc",), tag="rstd")

                def e(eng, sl=sl):
                    return eng.tensor_scalar(xn_bf[:], u_t[sl][:], ln_c[:, 2:3],
                                             ln_c[:, 7:8], ALU.subtract, ALU.mult)
                sch.add("dve", e, reads=(us, "lnc"), writes=("xnbf",), tag="xn")

                if "d_xnbf" in dbg:
                    def e(eng, t0=t0, g=g):
                        return eng.dma_start(
                            out=dbg["d_xnbf"][t0 + 128 * g:t0 + 128 * (g + 1), :],
                            in_=xn_bf[:])
                    sch.add("gd", e, reads=("xnbf",), writes=("dxnbf",), tag="dxnbf")
                if "d_lnc" in dbg:
                    def e(eng, t0=t0, g=g, tile=tile):
                        return eng.dma_start(
                            out=dbg["d_lnc"][tile * 8 + g], in_=ln_c[:])
                    sch.add("gd", e, reads=("lnc",), writes=("dlnc",), tag="dlnc")
                for r in range(2):
                    def e(eng, r=r):
                        last = None
                        for k in range(8):
                            hc = 8 * r + k
                            last = eng.transpose(
                                PT[:, 128 * k:128 * (k + 1)],
                                xn_bf[:, 128 * hc:128 * (hc + 1)], c_idb[:])
                        return last
                    sch.add("pe", e, reads=("xnbf", "c_idb"), writes=("psb7",),
                            tag=f"xnT{g}.{r}")

                    def e(eng, r=r, g=g):
                        return eng.copy(
                            xn_fm[:, 8 * r:8 * (r + 1), 128 * g:128 * (g + 1)],
                            PT[:].rearrange("p (a b) -> p a b", a=8))
                    sch.add("act", e, reads=("psb7",), writes=("xnfm",),
                            tag=f"xnTe{g}.{r}")

            for hc in range(16):
                def e(eng, hc=hc, t0=t0):
                    return eng.dma_start(out=xn_s[hc][:, t0:t0 + TT1],
                                         in_=xn_fm[:, hc, :])
                sch.add("gd", e, reads=("xnfm",), writes=("xn_s",), tag="xnstore")

            for cb in range(17):
                wsl = cb % 2
                wres = f"win{wsl}"
                ncols = 512 if cb < 16 else 320
                c0 = cb * 512
                for half in range(2):
                    def e(eng, wsl=wsl, c0=c0, ncols=ncols, half=half):
                        h0 = half * 8
                        return eng.dma_start(
                            out=win_t[wsl][:, h0:h0 + 8, 0:ncols],
                            in_=win_b.rearrange("(hc p) c -> p hc c", p=128)
                            [:, h0:h0 + 8, c0:c0 + ncols])
                    sch.add("sp", e, reads=("win_b",), writes=(wres,),
                            tag=f"winld{tile}.{cb}.{half}")

                if cb < 8:
                    for g in range(8):
                        bk = (cb * 8 + g) % 6
                        def e(eng, g=g, bk=bk, wsl=wsl):
                            last = None
                            for hc in range(16):
                                last = eng.matmul(
                                    PSB(bk),
                                    xn_fm[:, hc, 128 * g:128 * (g + 1)],
                                    win_t[wsl][:, hc, :],
                                    start=(hc == 0), stop=(hc == 15))
                            return last
                        sch.add("pe", e, reads=("xnfm", wres),
                                writes=(f"psb{bk}",), tag=f"zmm{cb}.{g}")

                        zsl = g % 2
                        def e(eng, bk=bk, zsl=zsl):
                            return eng.activation(zst[zsl][:], PSB(bk), AF.Silu)
                        sch.add("act", e, reads=(f"psb{bk}",),
                                writes=(f"zst{zsl}",), tag=f"zev{cb}.{g}")

                        def e(eng, t0=t0, g=g, cb=cb, zsl=zsl):
                            return eng.dma_start(
                                out=siluz_s[t0 + 128 * g:t0 + 128 * (g + 1),
                                            512 * cb:512 * (cb + 1)],
                                in_=zst[zsl][:])
                        sch.add("gd", e, reads=(f"zst{zsl}",),
                                writes=("siluz_s",), tag=f"zst{cb}.{g}")
                else:
                    nsub = 4 if cb < 16 else 2
                    for m in range(nsub):
                        ch = (cb - 8) * 4 + m
                        for half in range(2):
                            bk = (ch * 2 + half) % 6
                            def e(eng, m=m, half=half, bk=bk, wsl=wsl):
                                last = None
                                for hc in range(16):
                                    last = eng.matmul(
                                        PSB(bk),
                                        win_t[wsl][:, hc, 128 * m:128 * (m + 1)],
                                        xn_fm[:, hc, 512 * half:512 * (half + 1)],
                                        start=(hc == 0), stop=(hc == 15))
                                return last
                            sch.add("pe", e, reads=("xnfm", wres),
                                    writes=(f"psb{bk}",), tag=f"xmm{cb}.{m}.{half}")

                            xsl = half
                            eng_key = "act" if half == 0 else "dve"
                            def e(eng, bk=bk, xsl=xsl):
                                return eng.tensor_copy(xbst[xsl][:], PSB(bk)) \
                                    if not hasattr(eng, "activation") else \
                                    eng.activation(xbst[xsl][:], PSB(bk), AF.Copy)
                            def e(eng, bk=bk, xsl=xsl):
                                return eng.tensor_copy(xbst[xsl][:], PSB(bk))
                            sch.add("dve", e, reads=(f"psb{bk}",),
                                    writes=(f"xbst{xsl}",), tag=f"xev{cb}.{m}.{half}")

                            def e(eng, ch=ch, half=half, t0=t0, xsl=xsl):
                                return eng.dma_start(
                                    out=xbc_s[ch][:, t0 + 512 * half:
                                                  t0 + 512 * (half + 1)],
                                    in_=xbst[xsl][:])
                            sch.add("gd", e, reads=(f"xbst{xsl}",),
                                    writes=("xbc_s",), tag=f"xst{cb}.{m}.{half}")
                    if cb == 16:
                        for g in range(8):
                            def e(eng, g=g, wsl=wsl):
                                last = None
                                for hc in range(16):
                                    last = eng.matmul(
                                        PS[:, 6 * 512 + 64 * g:6 * 512 + 64 * (g + 1)],
                                        xn_fm[:, hc, 128 * g:128 * (g + 1)],
                                        win_t[wsl][:, hc, 256:320],
                                        start=(hc == 0), stop=(hc == 15))
                                return last
                            sch.add("pe", e, reads=("xnfm", wres),
                                    writes=("psb6",), tag=f"dtmm{tile}.{g}")

                        def e(eng, tile=tile):
                            return eng.tensor_copy(
                                dt_tm[:, 512 * tile:512 * (tile + 1)],
                                PS[:, 6 * 512:7 * 512])
                        sch.add("dve", e, reads=("psb6",), writes=("dttm",),
                                tag=f"dtev{tile}")

    sch.barrier()

    with ExitStack() as s2es:
        def sb2(name, shape, dty=F32):
            return s2es.enter_context(nc.sbuf_tensor(name, shape, dty))
        xbc_t = sb2("xbc_t", [128, 34, 520], BF16)
        cacc = [sb2(f"cacc{i}", [128, 512]) for i in range(2)]
        x_tm = sb2("x_tm", [128, DI], BF16)
        b_tm = sb2("b_tm", [128, 128], BF16)
        sszt = sb2("sszt", [128, DI], BF16)
        dcol = sb2("dcol", [128, 6 * NH])
        cumcl = sb2("cumcl", [128, 128])
        ecol = sb2("ecol", [128, 3 * NH])
        rcol = sb2("rcol", [128, 4])
        XW = sb2("XW", [128, DI], BF16)
        XW2 = sb2("XW2", [128, DI], BF16)
        scrA = sb2("scrA", [128, DI])
        scrB = sb2("scrB", [128, DI])
        Mbf = sb2("Mbf", [128, NH, 128], BF16)
        prevA = sb2("prevA", [128, DI])
        prevB = sb2("prevB", [128, DI])
        prevbf = sb2("prevbf", [128, DI], BF16)
        CBt = sb2("CBt", [128, 128])
        tbf = sb2("tbf", [128, DI], BF16)
        tfmc = sb2("tfmc", [128, 32, 128], BF16)
        c_normw = sb2("c_normw", [128, DI])

        sch.add("sp", lambda eng: eng.dma_start(out=c_normw[:], in_=normwbc),
                writes=("c_normw",), tag="normw")
        sch.add("dve", lambda eng: eng.memset(prevA[:], 0.0),
                writes=("prevA",), tag="previnit")

        def D(key, emit, reads=(), writes=(), tag=""):
            sch.add(key, emit, reads, writes, tag)

        def snap(name, ap, res):
            if name in dbg:
                def e(eng, name=name, ap=ap):
                    return eng.dma_start(out=dbg[name], in_=ap)
                sch.add("gd", e, reads=(res,), writes=(), tag=f"snap_{name}")

        dc = lambda i: dcol[:, NH * i:NH * (i + 1)]
        ec = lambda i: ecol[:, NH * i:NH * (i + 1)]

        for st5 in range(NT5):
            t0 = st5 * TT5
            for ch in range(34):
                res = f"xbct{ch}"
                if t0 == 0:
                    def e(eng, ch=ch):
                        return eng.dma_start(out=xbc_t[:, ch, 4:516],
                                             in_=xbc_s[ch][:, 0:512])
                    D("sp", e, reads=("xbc_s",), writes=(res,), tag=f"xld{st5}.{ch}")
                    def e(eng, ch=ch):
                        return eng.memset(xbc_t[:, ch, 0:4], 0.0)
                    D("dve", e, writes=(res,), tag=f"xpad{ch}")
                else:
                    def e(eng, ch=ch, t0=t0):
                        return eng.dma_start(out=xbc_t[:, ch, 0:516],
                                             in_=xbc_s[ch][:, t0 - 4:t0 + 512])
                    D("sp", e, reads=("xbc_s",), writes=(res,), tag=f"xld{st5}.{ch}")
                if st5 == 0 and ch == 0:
                    snap("d_xpre", xbc_t[:, 0, :], "xbct0")
                ca, cbk = f"cacc{0}", f"cacc{1}"
                def e(eng, ch=ch):
                    return eng.tensor_scalar(cacc[0][:], xbc_t[:, ch, 1:513],
                                             c_convw[:, 4 * ch:4 * ch + 1], None,
                                             ALU.mult)
                D("dve", e, reads=(res, "c_convw"), writes=(ca,), tag=f"cv0.{ch}")
                for k in (1, 2):
                    def e(eng, ch=ch, k=k):
                        return eng.scalar_tensor_tensor(
                            cacc[k % 2][:], xbc_t[:, ch, 1 + k:1 + k + 512],
                            c_convw[:, 4 * ch + k:4 * ch + k + 1],
                            cacc[(k + 1) % 2][:], ALU.mult, ALU.add)
                    D("dve", e, reads=(res, "c_convw", (ca if k % 2 == 1 else cbk)),
                      writes=((cbk if k % 2 == 1 else ca),), tag=f"cv{k}.{ch}")
                def e(eng, ch=ch):
                    return eng.scalar_tensor_tensor(
                        cacc[1][:], xbc_t[:, ch, 4:516],
                        c_convw[:, 4 * ch + 3:4 * ch + 4],
                        cacc[0][:], ALU.mult, ALU.add)
                D("dve", e, reads=(res, "c_convw", ca), writes=(cbk,), tag=f"cv3.{ch}")
                if st5 == 0 and ch == 0:
                    snap("d_cacc", cacc[1][:], "cacc1")
                    snap("d_cw", c_convw[:], "c_convw")
                    snap("d_cb", c_convb[:], "c_convb")
                def e(eng, ch=ch):
                    return eng.activation(xbc_t[:, ch, 4:516], cacc[1][:], AF.Silu,
                                          bias=c_convb[:, ch:ch + 1])
                D("act", e, reads=(cbk, "c_convb", res), writes=(res,),
                  tag=f"cvs.{ch}")

            if st5 == 0 and "d_xcv" in dbg:
                for ch2 in range(34):
                    def e(eng, ch2=ch2):
                        return eng.dma_start(out=dbg["d_xcv"][:, ch2, :],
                                             in_=xbc_t[:, ch2, 4:516])
                    sch.add("gd", e, reads=(f"xbct{ch2}",), writes=(), tag="dxcv")
            for loc in range(4):
                c = st5 * 4 + loc
                col0 = 4 + loc * 128

                def e(eng, c=c):
                    return eng.dma_start(out=sszt[:],
                                         in_=siluz_s[128 * c:128 * (c + 1), :])
                D("sp", e, reads=("siluz_s",), writes=("sszt",), tag=f"szld{c}")

                def e(eng, c=c):
                    return eng.tensor_add(dc(0), dt_tm[:, NH * c:NH * (c + 1)],
                                          c_dtb[:])
                D("dve", e, reads=("dttm", "c_dtb"), writes=("dcol",), tag="sp_v")
                D("act", lambda eng: eng.activation(dc(1), dc(0), AF.Abs),
                  reads=("dcol",), writes=("dcol",), tag="sp_abs")
                D("act", lambda eng: eng.activation(dc(2), dc(1), AF.Exp, scale=-1.0),
                  reads=("dcol",), writes=("dcol",), tag="sp_exp")
                D("act", lambda eng: eng.activation(dc(1), dc(2), AF.Ln, bias=1.0),
                  reads=("dcol",), writes=("dcol",), tag="sp_ln")
                D("dve", lambda eng: eng.tensor_scalar_max(dc(2), dc(0), 0.0),
                  reads=("dcol",), writes=("dcol",), tag="sp_max")
                D("dve", lambda eng: eng.tensor_add(dc(4), dc(1), dc(2)),
                  reads=("dcol",), writes=("dcol",), tag="sp_dtf")
                D("dve", lambda eng: eng.tensor_mul(dc(5), dc(4), c_nA[:]),
                  reads=("dcol", "c_nA"), writes=("dcol",), tag="sp_da")

                if c == 0:
                    snap("d_dcol", dcol[:], "dcol")
                def e(eng):
                    eng.matmul(PS[:, 0:64], c_tri[:], dc(5), start=True, stop=True)
                    return eng.matmul(PS[:, 64:128], c_ones[:], dc(5),
                                      start=True, stop=True)
                D("pe", e, reads=("dcol", "c_tri", "c_ones"), writes=("psb0",),
                  tag=f"cumm{c}")
                D("dve", lambda eng: eng.tensor_copy(cumcl[:], PS[:, 0:128]),
                  reads=("psb0",), writes=("cumcl",), tag="cumev")
                def e(eng):
                    return eng.tensor_sub(ec(0), cumcl[:, 64:128], cumcl[:, 0:64])
                D("dve", e, reads=("cumcl",), writes=("ecol",), tag="dl0")
                D("act", lambda eng: eng.activation(ec(0), ec(0), AF.Exp),
                  reads=("ecol",), writes=("ecol",), tag="dlast")
                def e(eng):
                    return eng.activation(ec(1), cumcl[:, 64:128], AF.Exp)
                D("act", e, reads=("cumcl",), writes=("ecol",), tag="atot")
                def e(eng):
                    return eng.activation(ec(2), cumcl[:, 0:64], AF.Exp)
                D("act", e, reads=("cumcl",), writes=("ecol",), tag="expcum")

                if c == 0:
                    snap("d_cumcl", cumcl[:], "cumcl")
                    snap("d_ecol", ecol[:], "ecol")
                for r in range(4):
                    def e(eng, r=r, col0=col0):
                        last = None
                        for k in range(8):
                            cc = 8 * r + k
                            last = eng.transpose(PT[:, 128 * k:128 * (k + 1)],
                                                 xbc_t[:, cc, col0:col0 + 128],
                                                 c_idb[:])
                        return last
                    D("pe", e,
                      reads=tuple(f"xbct{8 * r + k}" for k in range(8)) + ("c_idb",),
                      writes=("psb7",), tag=f"xT{c}.{r}")
                    def e(eng, r=r):
                        return eng.copy(
                            x_tm[:, 1024 * r:1024 * (r + 1)], PT[:])
                    D("act", e, reads=("psb7",), writes=("x_tm",), tag=f"xTe{c}.{r}")
                def e(eng, col0=col0):
                    return eng.transpose(PT[:, 0:128],
                                         xbc_t[:, 32, col0:col0 + 128], c_idb[:])
                D("pe", e, reads=("xbct32", "c_idb"), writes=("psb7",), tag=f"bT{c}")
                D("act", lambda eng: eng.copy(b_tm[:], PT[:, 0:128]),
                  reads=("psb7",), writes=("b_tm",), tag=f"bTe{c}")

                def e(eng):
                    return eng.tensor_tensor(
                        XW[:].rearrange("p (h q) -> p h q", h=NH),
                        x_tm[:].rearrange("p (h q) -> p h q", h=NH),
                        dc(4).unsqueeze(2).broadcast_to([128, NH, HD]), ALU.mult)
                D("dve", e, reads=("x_tm", "dcol"), writes=("XW",), tag=f"xw{c}")
                def e(eng):
                    return eng.tensor_tensor(
                        XW2[:].rearrange("p (h q) -> p h q", h=NH),
                        XW[:].rearrange("p (h q) -> p h q", h=NH),
                        ec(0).unsqueeze(2).broadcast_to([128, NH, HD]), ALU.mult)
                D("dve", e, reads=("XW", "ecol"), writes=("XW2",), tag=f"xw2{c}")

                if c == 0:
                    snap("d_xtm", x_tm[:], "x_tm")
                def e(eng, col0=col0):
                    return eng.matmul(PS[:, 512:640], xbc_t[:, 32, col0:col0 + 128],
                                      xbc_t[:, 33, col0:col0 + 128],
                                      start=True, stop=True)
                D("pe", e, reads=("xbct32", "xbct33"), writes=("psb1",),
                  tag=f"cbmm{c}")
                D("dve", lambda eng: eng.tensor_copy(CBt[:], PS[:, 512:640]),
                  reads=("psb1",), writes=("CBt",), tag="cbev")

                for q in range(4):
                    h0 = 16 * q
                    def e(eng, h0=h0):
                        return eng.affine_select(
                            scrA[:, 0:2048].rearrange("p (h i) -> p h i", h=16),
                            dc(5)[:, h0:h0 + 16].unsqueeze(2)
                            .broadcast_to([128, 16, 128]),
                            [[0, 16], [1, 128]], ALU.is_ge, 0.0,
                            channel_multiplier=-1)
                    D("pool", e, reads=("dcol",), writes=("scrA",), tag=f"rmask{c}.{q}")
                    def e(eng):
                        last = None
                        for n in range(4):
                            last = eng.matmul(PSB(n), c_ones[:],
                                              scrA[:, 512 * n:512 * (n + 1)],
                                              start=True, stop=True)
                        return last
                    D("pe", e, reads=("scrA", "c_ones"),
                      writes=("psb0", "psb1", "psb2", "psb3"), tag=f"cbc{c}.{q}")
                    def e(eng, h0=h0):
                        return eng.tensor_sub(
                            scrB[:, 0:2048].rearrange("p (h i) -> p h i", h=16),
                            PS[:, 0:2048].rearrange("p (h i) -> p h i", h=16),
                            cumcl[:, h0:h0 + 16].unsqueeze(2)
                            .broadcast_to([128, 16, 128]))
                    D("dve", e, reads=("psb0", "psb1", "psb2", "psb3", "cumcl"),
                      writes=("scrB",), tag=f"seg{c}.{q}")
                    D("act", lambda eng: eng.activation(scrA[:, 0:2048],
                                                        scrB[:, 0:2048], AF.Exp),
                      reads=("scrB",), writes=("scrA",), tag=f"dexp{c}.{q}")
                    def e(eng):
                        return eng.affine_select(
                            scrB[:, 0:2048].rearrange("p (h i) -> p h i", h=16),
                            scrA[:, 0:2048].rearrange("p (h i) -> p h i", h=16),
                            [[0, 16], [1, 128]], ALU.is_ge, 0.0,
                            channel_multiplier=-1)
                    D("pool", e, reads=("scrA",), writes=("scrB",), tag=f"dmask{c}.{q}")
                    def e(eng):
                        return eng.tensor_tensor(
                            scrA[:, 0:2048].rearrange("p (h i) -> p h i", h=16),
                            scrB[:, 0:2048].rearrange("p (h i) -> p h i", h=16),
                            CBt[:].unsqueeze(1).broadcast_to([128, 16, 128]),
                            ALU.mult)
                    D("dve", e, reads=("scrB", "CBt"), writes=("scrA",),
                      tag=f"mcb{c}.{q}")
                    def e(eng, h0=h0):
                        return eng.tensor_copy(
                            Mbf[:, h0:h0 + 16, :],
                            scrA[:, 0:2048].rearrange("p (h i) -> p h i", h=16))
                    D("dve", e, reads=("scrA",), writes=("Mbf",),
                      tag=f"mcast{c}.{q}")

                if c == 0:
                    snap("d_cbt", CBt[:], "CBt")
                    snap("d_mbf", Mbf[:].rearrange("p a b -> p (a b)"), "Mbf")
                D("dve", lambda eng: eng.tensor_copy(prevbf[:], prevA[:]),
                  reads=("prevA",), writes=("prevbf",), tag=f"pbf{c}")
                def e(eng):
                    return eng.tensor_tensor(
                        prevB[:].rearrange("p (h q) -> p h q", h=NH),
                        prevA[:].rearrange("p (h q) -> p h q", h=NH),
                        ec(1).unsqueeze(2).broadcast_to([128, NH, HD]), ALU.mult)
                D("dve", e, reads=("prevA", "ecol"), writes=("prevB",), tag=f"pB{c}")

                for hh in range(2):
                    o0 = 2048 * hh
                    h0 = 32 * hh
                    def e(eng, o0=o0, col0=col0):
                        last = None
                        for n in range(4):
                            last = eng.matmul(PSB(n),
                                              xbc_t[:, 33, col0:col0 + 128],
                                              prevbf[:, o0 + 512 * n:o0 + 512 * (n + 1)],
                                              start=True, stop=True)
                        return last
                    D("pe", e, reads=("xbct33", "prevbf"),
                      writes=("psb0", "psb1", "psb2", "psb3"), tag=f"yoff{c}.{hh}")
                    def e(eng, o0=o0, h0=h0):
                        return eng.tensor_tensor(
                            scrB[:, o0:o0 + 2048].rearrange("p (h q) -> p h q", h=32),
                            PS[:, 0:2048].rearrange("p (h q) -> p h q", h=32),
                            ec(2)[:, h0:h0 + 32].unsqueeze(2)
                            .broadcast_to([128, 32, HD]), ALU.mult)
                    D("dve", e, reads=("psb0", "psb1", "psb2", "psb3", "ecol"),
                      writes=("scrB",), tag=f"yoffev{c}.{hh}")
                    def e(eng, o0=o0):
                        last = None
                        for n in range(4):
                            last = eng.matmul(PSB(n), b_tm[:],
                                              XW2[:, o0 + 512 * n:o0 + 512 * (n + 1)],
                                              start=True, stop=True)
                        return last
                    D("pe", e, reads=("b_tm", "XW2"),
                      writes=("psb0", "psb1", "psb2", "psb3"), tag=f"stat{c}.{hh}")
                    def e(eng, o0=o0):
                        return eng.tensor_add(prevA[:, o0:o0 + 2048],
                                              prevB[:, o0:o0 + 2048],
                                              PS[:, 0:2048])
                    D("dve", e, reads=("prevB", "psb0", "psb1", "psb2", "psb3"),
                      writes=("prevA",), tag=f"pup{c}.{hh}")
                    def e(eng, h0=h0):
                        last = None
                        for k in range(32):
                            h = h0 + k
                            last = eng.matmul(
                                PS[:, 64 * k:64 * (k + 1)],
                                Mbf[:, h, :], XW[:, 64 * h:64 * (h + 1)],
                                start=True, stop=True)
                        return last
                    D("pe", e, reads=("Mbf", "XW"),
                      writes=("psb0", "psb1", "psb2", "psb3"), tag=f"ydiag{c}.{hh}")
                    def e(eng, o0=o0, h0=h0):
                        return eng.tensor_tensor(
                            scrA[:, o0:o0 + 2048].rearrange("p (h q) -> p h q", h=32),
                            x_tm[:, o0:o0 + 2048].rearrange("p (h q) -> p h q", h=32),
                            c_dbc[:, h0:h0 + 32].unsqueeze(2)
                            .broadcast_to([128, 32, HD]), ALU.mult)
                    D("dve", e, reads=("x_tm", "c_dbc"), writes=("scrA",),
                      tag=f"yxd{c}.{hh}")
                    def e(eng, o0=o0):
                        return eng.tensor_add(scrA[:, o0:o0 + 2048],
                                              scrA[:, o0:o0 + 2048],
                                              scrB[:, o0:o0 + 2048])
                    D("dve", e, reads=("scrA", "scrB"), writes=("scrA",),
                      tag=f"yadd{c}.{hh}")
                    def e(eng, o0=o0):
                        return eng.tensor_add(scrA[:, o0:o0 + 2048],
                                              scrA[:, o0:o0 + 2048],
                                              PS[:, 0:2048])
                    D("dve", e, reads=("scrA", "psb0", "psb1", "psb2", "psb3"),
                      writes=("scrA",), tag=f"ydg{c}.{hh}")

                if c == 0:
                    snap("d_yoff", scrB[:], "scrB")
                    snap("d_y", scrA[:], "scrA")
                    snap("d_prev", prevA[:], "prevA")
                def e(eng):
                    return eng.tensor_mul(scrA[:], scrA[:], sszt[:])
                D("dve", e, reads=("scrA", "sszt"), writes=("scrA",), tag=f"tz{c}")
                def e(eng):
                    return eng.activation(scrB[:], scrA[:], AF.Square)
                D("act", e, reads=("scrA",), writes=("scrB",), tag=f"rms{c}")
                def e(eng):
                    return eng.tensor_reduce(rcol[:, 0:1], scrB[:], AX.X, ALU.add)
                D("dve", e, reads=("scrB",), writes=("rcol",), tag=f"rmsr{c}")
                def e(eng):
                    return eng.tensor_scalar(rcol[:, 1:2], rcol[:, 0:1],
                                             1.0 / DI, EPS, ALU.mult, ALU.add)
                D("dve", e, reads=("rcol",), writes=("rcol",), tag="rms2")
                D("act", lambda eng: eng.activation(rcol[:, 2:3], rcol[:, 1:2],
                                                    AF.Ln),
                  reads=("rcol",), writes=("rcol",), tag="rmsln")
                D("dve", lambda eng: eng.tensor_scalar_mul(rcol[:, 1:2],
                                                           rcol[:, 2:3], -0.5),
                  reads=("rcol",), writes=("rcol",), tag="rmsh")
                D("act", lambda eng: eng.activation(rcol[:, 3:4], rcol[:, 1:2],
                                                    AF.Exp),
                  reads=("rcol",), writes=("rcol",), tag="rstd")
                def e(eng):
                    return eng.tensor_scalar(scrA[:], scrA[:], rcol[:, 3:4], None,
                                             ALU.mult)
                D("dve", e, reads=("scrA", "rcol"), writes=("scrA",), tag=f"trs{c}")
                def e(eng):
                    return eng.tensor_mul(tbf[:], scrA[:], c_normw[:])
                D("dve", e, reads=("scrA", "c_normw"), writes=("tbf",), tag=f"tnw{c}")

                if c == 0:
                    snap("d_tbf", tbf[:], "tbf")
                for r in range(4):
                    def e(eng, r=r):
                        last = None
                        for k in range(8):
                            cc = 8 * r + k
                            last = eng.transpose(PT[:, 128 * k:128 * (k + 1)],
                                                 tbf[:, 128 * cc:128 * (cc + 1)],
                                                 c_idb[:])
                        return last
                    D("pe", e, reads=("tbf", "c_idb"), writes=("psb7",),
                      tag=f"tT{c}.{r}")
                    def e(eng, r=r):
                        return eng.copy(tfmc[:, 8 * r:8 * (r + 1), :],
                                        PT[:].rearrange("p (a b) -> p a b", a=8))
                    D("act", e, reads=("psb7",), writes=("tfmc",), tag=f"tTe{c}.{r}")
                def e(eng, c=c):
                    return eng.dma_start(out=tfm_s[c], in_=tfmc[:])
                D("gd", e, reads=("tfmc",), writes=("tfm_s",), tag=f"tst{c}")

    sch.barrier()

    with ExitStack() as s5es:
        def sb5(name, shape, dty=F32):
            return s5es.enter_context(nc.sbuf_tensor(name, shape, dty))
        xn5 = sb5("xn5", [128, 16, TT5], BF16)
        AR1 = sb5("AR1", [128, 16, IE], BF16)
        AR2 = sb5("AR2", [128, 16, IE], BF16)
        BRu = sb5("BRu", [128, 11 * DM], BF16)
        ACC = sb5("ACC", [128, 4, DM])
        inter = sb5("inter", [128, 11, TT5], BF16)
        sgt = sb5("sgt", [128, TT5])
        gwT = sb5("gwT", [128, 16, NE], BF16)
        sco = sb5("sco", [128, 4, NE])
        gat = sb5("gat", [128, 4, 24])
        wrk = sb5("wrk", [128, 4, NE])
        sel = sb5("sel", [128, 4, NE])
        comb = sb5("comb", [128, 4, NE])
        m8g = sb5("m8g", [128, 4, 8])
        dpT_v = BRu[:].rearrange("p (a b) -> p a b", a=11)
        tfm_v = AR1[:].rearrange("p a b -> p (a b)")[:, 0:16384] \
            .rearrange("p (a b) -> p a b", a=32)
        wo_v = AR2[:].rearrange("p a b -> p (a b)")[:, 0:16384] \
            .rearrange("p (a b) -> p a b", a=8)

        c_ebias = sb5("c_ebias", [128, NE])
        c_efac = sb5("c_efac", [128, NE])

        def e(eng):
            return eng.dma_start(out=gwT[:],
                                 in_=gwTf.rearrange("(hc p) e -> p hc e", p=128))
        sch.add("gd", e, writes=("gwT",), tag="gwT")
        sch.add("sp", lambda eng: eng.dma_start(out=c_ebias[:], in_=ebiasbc),
                writes=("c_ebias",), tag="ebias")
        sch.add("sp", lambda eng: eng.dma_start(out=c_efac[:], in_=efacbc),
                writes=("c_efac",), tag="efac")

        bank_i = [0]
        def rot():
            b = bank_i[0] % 6
            bank_i[0] += 1
            return b

        def D(key, emit, reads=(), writes=(), tag=""):
            sch.add(key, emit, reads, writes, tag)

        def snap5(name, ap, res):
            if name in dbg:
                def e(eng, name=name, ap=ap):
                    return eng.dma_start(out=dbg[name], in_=ap)
                sch.add("gd", e, reads=(res,), writes=(), tag=f"snap_{name}")

        for st5 in range(NT5):
            t0 = st5 * TT5
            for hc in range(16):
                def e(eng, hc=hc, t0=t0):
                    return eng.dma_start(out=xn5[:, hc, :],
                                         in_=xn_s[hc][:, t0:t0 + TT5])
                D("sp", e, reads=("xn_s",), writes=("xn5",), tag=f"xnld{st5}")
            for g in range(4):
                def e(eng, g=g, t0=t0):
                    return eng.dma_start(
                        out=ACC[:, g, :],
                        in_=u[t0 + 128 * g:t0 + 128 * (g + 1), :])
                D("sp", e, writes=("ACC",), tag=f"accini{st5}.{g}")

            for g in range(4):
                def e(eng, g=g):
                    last = None
                    for hc in range(16):
                        last = eng.matmul(
                            PS[:, 6 * 512 + 16 * g:6 * 512 + 16 * (g + 1)],
                            xn5[:, hc, 128 * g:128 * (g + 1)], gwT[:, hc, :],
                            start=(hc == 0), stop=(hc == 15))
                    return last
                D("pe", e, reads=("xn5", "gwT"), writes=("psb6",), tag=f"glog{st5}")
            def e(eng):
                return eng.activation(sco[:], PS[:, 6 * 512:6 * 512 + 64],
                                      AF.Sigmoid)
            D("act", e, reads=("psb6",), writes=("sco",), tag=f"gsig{st5}")
            def e(eng):
                return eng.tensor_tensor(
                    wrk[:], sco[:],
                    c_ebias[:].unsqueeze(1).broadcast_to([128, 4, NE]), ALU.add)
            D("dve", e, reads=("sco", "c_ebias"), writes=("wrk",), tag="sfc")
            sfc4 = wrk[:].rearrange("p g (a b) -> p g a b", a=4)
            pairs = [(0, 1), (0, 2), (0, 3), (1, 2), (1, 3), (2, 3)]
            for pi, (i1, i2) in enumerate(pairs):
                def e(eng, i1=i1, i2=i2, pi=pi):
                    dst = gat[:, :, 4:8] if pi % 2 else gat[:, :, 0:4]
                    return eng.tensor_tensor(dst, sfc4[:, :, :, i1],
                                             sfc4[:, :, :, i2], ALU.add)
                if pi < 2:
                    D("dve", e, reads=("wrk",), writes=("gat",), tag=f"gp{pi}")
                else:
                    def e2(eng, i1=i1, i2=i2):
                        return eng.tensor_tensor(gat[:, :, 8:12],
                                                 sfc4[:, :, :, i1],
                                                 sfc4[:, :, :, i2], ALU.add)
                    D("dve", e2, reads=("wrk",), writes=("gat",), tag=f"gpx")
                    def e3(eng):
                        return eng.tensor_max(gat[:, :, 0:4], gat[:, :, 0:4],
                                              gat[:, :, 8:12])
                    D("dve", e3, reads=("gat",), writes=("gat",), tag=f"gmx")
            def e(eng):
                return eng.tensor_max(gat[:, :, 0:4], gat[:, :, 0:4],
                                      gat[:, :, 4:8])
            D("dve", e, reads=("gat",), writes=("gat",), tag="gmx2")
            def e(eng):
                return eng.tensor_reduce(gat[:, :, 12:13], gat[:, :, 0:4],
                                         AX.X, ALU.max)
            D("dve", e, reads=("gat",), writes=("gat",), tag="gm1")
            def e(eng):
                return eng.tensor_tensor(
                    gat[:, :, 16:20], gat[:, :, 0:4],
                    gat[:, :, 12:13].broadcast_to([128, 4, 4]), ALU.is_ge)
            D("dve", e, reads=("gat",), writes=("gat",), tag="gtie")
            def e(eng):
                return eng.scalar_tensor_tensor(gat[:, :, 16:20], gat[:, :, 16:20],
                                                -1e30, gat[:, :, 0:4],
                                                ALU.mult, ALU.add)
            D("dve", e, reads=("gat",), writes=("gat",), tag="grm")
            def e(eng):
                return eng.tensor_reduce(gat[:, :, 13:14], gat[:, :, 16:20],
                                         AX.X, ALU.max)
            D("dve", e, reads=("gat",), writes=("gat",), tag="gm2")
            def e(eng):
                return eng.tensor_tensor(
                    gat[:, :, 16:20], gat[:, :, 0:4],
                    gat[:, :, 13:14].broadcast_to([128, 4, 4]), ALU.is_ge)
            D("dve", e, reads=("gat",), writes=("gat",), tag="gmask")
            def e(eng):
                emask = gat[:, :, 16:20].unsqueeze(3).broadcast_to([128, 4, 4, 4])
                return eng.tensor_tensor(
                    wrk[:].rearrange("p g (a b) -> p g a b", a=4),
                    wrk[:].rearrange("p g (a b) -> p g a b", a=4),
                    emask, ALU.mult)
            D("dve", e, reads=("wrk", "gat"), writes=("wrk",), tag="emask")
            def e(eng):
                return eng.tensor_scalar_add(wrk[:], wrk[:], 2.0)
            D("dve", e, reads=("wrk",), writes=("wrk",), tag="w2")
            def e(eng):
                return eng.tensor_tensor(
                    wrk[:], wrk[:],
                    c_efac[:].unsqueeze(1).broadcast_to([128, 4, NE]), ALU.mult)
            D("dve", e, reads=("wrk", "c_efac"), writes=("wrk",), tag="wfac")
            for g in range(4):
                def e(eng, g=g):
                    return eng.max(out=m8g[:, g, :], in_=wrk[:, g, :])
                D("dve", e, reads=("wrk",), writes=("m8g",), tag=f"m8.{g}")
            def e(eng):
                return eng.tensor_tensor(
                    sel[:], wrk[:],
                    m8g[:, :, 5:6].broadcast_to([128, 4, NE]), ALU.is_ge)
            D("dve", e, reads=("wrk", "m8g"), writes=("sel",), tag="sel")
            def e(eng):
                return eng.tensor_tensor(sel[:], sel[:], sco[:], ALU.mult)
            D("dve", e, reads=("sel", "sco"), writes=("sel",), tag="wsel")
            def e(eng):
                return eng.tensor_reduce(gat[:, :, 14:15], sel[:], AX.X, ALU.add)
            D("dve", e, reads=("sel",), writes=("gat",), tag="gden")
            def e(eng):
                return eng.tensor_scalar_add(gat[:, :, 14:15], gat[:, :, 14:15],
                                             1e-20)
            D("dve", e, reads=("gat",), writes=("gat",), tag="gde2")
            def e(eng):
                return eng.reciprocal(gat[:, :, 15:16], gat[:, :, 14:15])
            D("dve", e, reads=("gat",), writes=("gat",), tag="grec")
            def e(eng):
                return eng.tensor_tensor(
                    comb[:], sel[:],
                    gat[:, :, 15:16].broadcast_to([128, 4, NE]), ALU.mult)
            D("dve", e, reads=("sel", "gat"), writes=("comb",), tag="comb1")
            def e(eng):
                return eng.tensor_scalar_mul(comb[:], comb[:], 2.5)
            D("dve", e, reads=("comb",), writes=("comb",), tag="comb2")
            if st5 == 0:
                snap5("d_sco", sco[:], "sco")
                snap5("d_comb", comb[:], "comb")

            for chc in range(32):
                def e(eng, chc=chc, st5=st5):
                    return eng.dma_start(
                        out=tfm_v[:, chc, :].rearrange("p (a b) -> p a b", a=4),
                        in_=tfm_s.rearrange("c p t -> p c t")
                        [:, st5 * 4:st5 * 4 + 4, 128 * chc:128 * (chc + 1)])
                D("sp", e, reads=("tfm_s",), writes=("AR1",), tag=f"tfmld{st5}")
            for cb8 in range(4):
                def e(eng, cb8=cb8):
                    return eng.dma_start(
                        out=wo_v[:],
                        in_=wout_b[1024 * cb8:1024 * (cb8 + 1), :]
                        .rearrange("(a p) n -> p a n", p=128))
                D("sp", e, reads=("wout_b",), writes=("AR2",), tag=f"wold{st5}.{cb8}")
                for g in range(4):
                    for nb in range(4):
                        bk = rot()
                        def e(eng, cb8=cb8, g=g, nb=nb, bk=bk):
                            last = None
                            for k in range(8):
                                chc = 8 * cb8 + k
                                last = eng.matmul(
                                    PSB(bk),
                                    tfm_v[:, chc, 128 * g:128 * (g + 1)],
                                    wo_v[:, k, 512 * nb:512 * (nb + 1)],
                                    start=(k == 0), stop=(k == 7))
                            return last
                        D("pe", e, reads=("AR1", "AR2"), writes=(f"psb{bk}",),
                          tag=f"womm{st5}.{cb8}.{g}.{nb}")
                        def e(eng, g=g, nb=nb, bk=bk):
                            return eng.tensor_add(
                                ACC[:, g, 512 * nb:512 * (nb + 1)],
                                ACC[:, g, 512 * nb:512 * (nb + 1)], PSB(bk))
                        D("dve", e, reads=("ACC", f"psb{bk}"), writes=("ACC",),
                          tag=f"woacc{st5}.{cb8}.{g}.{nb}")

            if st5 == 0:
                snap5("d_accw", ACC[:], "ACC")
                snap5("d_tfmv", tfm_v, "AR1")
            for ex in range(18):
                if ex < NE:
                    gsrc = gpT_b[ex].rearrange("(hc p) i -> p hc i", p=128)
                    usrc = upT_b[ex].rearrange("(hc p) i -> p hc i", p=128)
                    dsrc = dpT_b[ex].rearrange("(ic p) n -> p ic n", p=128)
                else:
                    half = ex - NE
                    gsrc = shgT_b[:, IE * half:IE * (half + 1)] \
                        .rearrange("(hc p) i -> p hc i", p=128)
                    usrc = shuT_b[:, IE * half:IE * (half + 1)] \
                        .rearrange("(hc p) i -> p hc i", p=128)
                    dsrc = shdT_b[IE * half:IE * (half + 1), :] \
                        .rearrange("(ic p) n -> p ic n", p=128)
                for hc in range(16):
                    def e(eng, hc=hc, gsrc=gsrc):
                        return eng.dma_start(out=AR1[:, hc, :], in_=gsrc[:, hc, :])
                    D("sp", e, reads=("gpT_b", "shgT_b"), writes=("AR1",),
                      tag=f"gld{st5}.{ex}")
                    def e(eng, hc=hc, usrc=usrc):
                        return eng.dma_start(out=AR2[:, hc, :], in_=usrc[:, hc, :])
                    D("sp", e, reads=("upT_b", "shuT_b"), writes=("AR2",),
                      tag=f"uld{st5}.{ex}")
                for ic in range(11):
                    def e(eng, ic=ic, dsrc=dsrc):
                        return eng.dma_start(out=dpT_v[:, ic, :],
                                             in_=dsrc[:, ic, :])
                    D("sp", e, reads=("dpT_b", "shdT_b"), writes=("BR",),
                      tag=f"dld{st5}.{ex}")
                for ic in range(11):
                    bka = rot()
                    bkb = rot()
                    def e(eng, ic=ic, bka=bka):
                        last = None
                        for hc in range(16):
                            last = eng.matmul(
                                PSB(bka), AR1[:, hc, 128 * ic:128 * (ic + 1)],
                                xn5[:, hc, :], start=(hc == 0), stop=(hc == 15))
                        return last
                    D("pe", e, reads=("AR1", "xn5"), writes=(f"psb{bka}",),
                      tag=f"gmm{st5}.{ex}.{ic}")
                    def e(eng, ic=ic, bkb=bkb):
                        last = None
                        for hc in range(16):
                            last = eng.matmul(
                                PSB(bkb), AR2[:, hc, 128 * ic:128 * (ic + 1)],
                                xn5[:, hc, :], start=(hc == 0), stop=(hc == 15))
                        return last
                    D("pe", e, reads=("AR2", "xn5"), writes=(f"psb{bkb}",),
                      tag=f"umm{st5}.{ex}.{ic}")
                    def e(eng, bka=bka):
                        return eng.activation(sgt[:], PSB(bka), AF.Silu)
                    D("act", e, reads=(f"psb{bka}",), writes=("sgt",),
                      tag=f"sil{st5}.{ex}.{ic}")
                    def e(eng, ic=ic, bkb=bkb):
                        return eng.tensor_tensor(inter[:, ic, :], sgt[:],
                                                 PSB(bkb), ALU.mult)
                    D("dve", e, reads=("sgt", f"psb{bkb}"), writes=("inter",),
                      tag=f"int{st5}.{ex}.{ic}")
                for g in range(4):
                    for nb in range(4):
                        bk = rot()
                        def e(eng, g=g, nb=nb, bk=bk):
                            last = None
                            for ic in range(11):
                                last = eng.matmul(
                                    PSB(bk),
                                    inter[:, ic, 128 * g:128 * (g + 1)],
                                    dpT_v[:, ic, 512 * nb:512 * (nb + 1)],
                                    start=(ic == 0), stop=(ic == 10))
                            return last
                        D("pe", e, reads=("inter", "BR"), writes=(f"psb{bk}",),
                          tag=f"dmm{st5}.{ex}.{g}.{nb}")
                        if ex < NE:
                            def e(eng, g=g, nb=nb, bk=bk, ex=ex):
                                return eng.scalar_tensor_tensor(
                                    ACC[:, g, 512 * nb:512 * (nb + 1)],
                                    PSB(bk), comb[:, g, ex:ex + 1],
                                    ACC[:, g, 512 * nb:512 * (nb + 1)],
                                    ALU.mult, ALU.add)
                            D("dve", e, reads=("ACC", "comb", f"psb{bk}"),
                              writes=("ACC",), tag=f"dacc{st5}.{ex}.{g}.{nb}")
                        else:
                            def e(eng, g=g, nb=nb, bk=bk):
                                return eng.tensor_add(
                                    ACC[:, g, 512 * nb:512 * (nb + 1)],
                                    ACC[:, g, 512 * nb:512 * (nb + 1)], PSB(bk))
                            D("dve", e, reads=("ACC", f"psb{bk}"),
                              writes=("ACC",), tag=f"sacc{st5}.{ex}.{g}.{nb}")

            if st5 == 0:
                snap5("d_acc0", ACC[:], "ACC")
            for g in range(4):
                def e(eng, g=g, t0=t0):
                    return eng.dma_start(
                        out=out[t0 + 128 * g:t0 + 128 * (g + 1), :],
                        in_=ACC[:, g, :])
                D("gd", e, reads=("ACC",), writes=("outd",), tag=f"ost{st5}.{g}")

    if dbg:
        sch.barrier()
        with ExitStack() as des:
            dstg = des.enter_context(nc.sbuf_tensor("dbg_t", [128, 4096], BF16))
            def dump(dst_ap, src_ap, n_rows_of_128, tagn):
                for i in range(n_rows_of_128):
                    def e(eng, i=i):
                        return eng.dma_start(out=dstg[:, 0:src_ap.shape[-1]],
                                             in_=src_ap[i] if src_ap.ndim == 3
                                             else src_ap[128 * i:128 * (i + 1), :])
                    sch.add("sp", e, reads=(), writes=("dbgt",), tag=tagn)
                    def e(eng, i=i):
                        return eng.dma_start(out=dst_ap[i] if dst_ap.ndim == 3
                                             else dst_ap[128 * i:128 * (i + 1), :],
                                             in_=dstg[:, 0:src_ap.shape[-1]])
                    sch.add("gd", e, reads=("dbgt",), writes=("dbgt",), tag=tagn)
            if "d_xn" in dbg:
                dump(dbg["d_xn"], xn_s, 16, "dxn")
            if "d_xbc" in dbg:
                dump(dbg["d_xbc"], xbc_s, 34, "dxbc")
            if "d_siluz" in dbg:
                dump(dbg["d_siluz"], siluz_s, 32, "dsz")
            if "d_tfm" in dbg:
                dump(dbg["d_tfm"], tfm_s, 32, "dtfm")
            if "d_dt" in dbg:
                dstg2 = des.enter_context(nc.sbuf_tensor("dbg_t2", [128, 2048], F32))
                def e(eng):
                    return eng.tensor_copy(dstg2[:], dt_tm[:])
                sch.add("dve", e, reads=("dttm",), writes=("dbg2",), tag="ddt")
                def e(eng):
                    return eng.dma_start(out=dbg["d_dt"], in_=dstg2[:])
                sch.add("gd", e, reads=("dbg2",), writes=("dbg2",), tag="ddt")

    sch.analyze()
    sems = {}
    for k in ("pe", "act", "dve", "pool", "sp", "gd", "ad"):
        sems[k] = es.enter_context(nc.semaphore(f"sem_{k}"))
    with nc.Block() as block:
        sch.emit(nc, block, sems)
    es.close()
    return nc


def _host_prep(inputs):
    f32 = np.float32
    gp = np.ascontiguousarray(inputs["L0_ffn_gate_proj"].transpose(0, 2, 1))
    up = np.ascontiguousarray(inputs["L0_ffn_up_proj"].transpose(0, 2, 1))
    dp = np.ascontiguousarray(inputs["L0_ffn_down_proj"].transpose(0, 2, 1))
    shg = np.ascontiguousarray(inputs["L0_ffn_shared_gate_w"].T)
    shu = np.ascontiguousarray(inputs["L0_ffn_shared_up_w"].T)
    shd = np.ascontiguousarray(inputs["L0_ffn_shared_down_w"].T)
    gwT = np.ascontiguousarray(inputs["L0_ffn_gate_weight"].T)
    cw = np.ascontiguousarray(
        inputs["L0_attn_conv_w"].reshape(34, 128, 4).transpose(1, 0, 2)
        .reshape(128, 136))
    cbv = np.ascontiguousarray(inputs["L0_attn_conv_b"].reshape(34, 128).T)
    rep = lambda v: np.ascontiguousarray(np.broadcast_to(v[None, :], (128, len(v))))
    common = dict(
        winf=np.asarray(inputs["L0_attn_W_in"], f32),
        woutf=np.asarray(inputs["L0_attn_W_out"], f32),
        gpTf=gp, upTf=up, dpTf=dp, shgTf=shg, shuTf=shu, shdTf=shd, gwTf=gwT,
        convw=cw, convb=cbv,
        dtbbc=rep(np.asarray(inputs["L0_attn_dt_bias"], f32)),
        alogbc=rep(np.asarray(inputs["L0_attn_A_log"], f32)),
        dbc=rep(np.asarray(inputs["L0_attn_D"], f32)),
        normwbc=rep(np.asarray(inputs["L0_attn_norm_w"], f32)),
        ebiasbc=rep(np.asarray(inputs["L0_ffn_e_score_correction_bias"], f32)),
        efacbc=rep((1.0 - np.arange(NE, dtype=f32) * 2.0 ** -21)),
        onessq=np.ones((128, 128), f32),
        tri=np.triu(np.ones((128, 128), f32)),
        idb=np.eye(128, dtype=ml_dtypes.bfloat16),
    )
    uf = np.asarray(inputs["L0_attn_u"], f32)
    return [dict(common, u=np.ascontiguousarray(uf[b])) for b in range(B)]


def kernel(**inputs):
    in_maps = _host_prep(inputs)
    nc = build()
    res = run_bass_kernel_spmd(nc, in_maps, core_ids=list(range(B)))
    return np.stack([np.asarray(r["out"], np.float32) for r in res.results], 0)


# version 07 (final)
# speedup vs baseline: 1.1307x; 1.1307x over previous
import bisect
import numpy as np
import ml_dtypes
from contextlib import ExitStack
from collections import defaultdict

import concourse.bass as bass
import concourse.mybir as mybir
from concourse.bass_utils import run_bass_kernel_spmd

F32 = mybir.dt.float32
BF16 = mybir.dt.bfloat16
AF = mybir.ActivationFunctionType
ALU = mybir.AluOpType
AX = mybir.AxisListType

B = 8
L = 4096
DM = 2048
DIN = 8512
DI = 4096
DCC = 4352
NH = 64
HD = 64
NST = 128
CH = 128
NCH = 32
NE = 16
IE = 1408
EPS = 1e-5

TT1 = 1024
NT1 = L // TT1
TT5 = 512
NT5 = L // TT5

INC = {"sp": 16, "gd": 16, "ad": 16}
PHYS = {"pe": "tensor", "act": "scalar", "dve": "vector", "pool": "gpsimd",
        "sp": "sync", "gd": "gpsimd", "ad": "scalar"}


class Sch:

    def __init__(self):
        self.ops = []

    def add(self, key, emit, reads=(), writes=(), tag=""):
        assert key in PHYS
        assert not isinstance(reads, str) and not isinstance(writes, str)
        for x in tuple(reads) + tuple(writes):
            assert isinstance(x, str) and len(x) > 1, (x, tag)
        self.ops.append([key, emit, tuple(reads), tuple(writes), tag])

    def barrier(self):
        self.ops.append(["BARRIER", None, (), (), "barrier"])

    SYNCKEYS = ("dve", "act", "pool")
    STRICT = False
    SEMI = True

    def analyze(self):
        seq = defaultdict(int)
        lastw = {}
        readers = defaultdict(list)
        clocks = defaultdict(dict)
        selfw = defaultdict(lambda: -1)
        snap = {}
        marks = defaultdict(set)
        self.deps = []
        bar = None
        for op in self.ops:
            key = op[0]
            if key == "BARRIER":
                bar = {k: seq[k] - 1 for k in seq if seq[k] > 0}
                self.deps.append(None)
                continue
            s = seq[key]
            seq[key] += 1
            clk = clocks[key]
            need = {}
            selfsync = key in self.SYNCKEYS or key in INC

            def nd(k2, s2):
                if k2 == key and not selfsync:
                    return
                if need.get(k2, -1) < s2:
                    need[k2] = s2
            if self.STRICT:
                for k2 in list(seq):
                    s2 = seq[k2] - 1 - (1 if k2 == key else 0)
                    if s2 >= 0 and k2 != key:
                        nd(k2, s2)
                    elif s2 >= 0 and key in self.SYNCKEYS:
                        nd(k2, s2)
            elif self.SEMI and key in ("pe", "act", "dve", "pool"):
                for k2 in ("pe", "act", "dve", "pool"):
                    s2 = seq[k2] - 1 - (1 if k2 == key else 0)
                    if s2 >= 0 and (k2 != key or key in self.SYNCKEYS):
                        nd(k2, s2)
            if key in INC and s > 0:
                nd(key, s - 1)
            if bar is not None:
                for k2, s2 in bar.items():
                    nd(k2, s2)
            for r in op[2]:
                lw = lastw.get(r)
                if lw:
                    nd(*lw)
            for w in op[3]:
                lw = lastw.get(w)
                if lw:
                    nd(*lw)
                for rd in readers.get(w, ()):
                    nd(*rd)
            waits = {}
            for k2, s2 in need.items():
                if k2 == key:
                    if selfw[key] < s2:
                        waits[k2] = s2
                        selfw[key] = s2
                else:
                    if clk.get(k2, -1) < s2:
                        waits[k2] = s2
            for k2, s2 in waits.items():
                sn = snap[(k2, s2)]
                for k3, s3 in sn.items():
                    if clk.get(k3, -1) < s3:
                        clk[k3] = s3
                marks[k2].add(s2)
            clk[key] = s
            snap[(key, s)] = dict(clk)
            self.deps.append(waits)
            for r in op[2]:
                readers[r].append((key, s))
            for w in op[3]:
                lastw[w] = (key, s)
                readers[w] = [(key, s)]
        for k in INC:
            if seq.get(k, 0):
                marks[k].update(range(seq[k]))
        self.marks = {k: sorted(v) for k, v in marks.items()}
        self.nseq = dict(seq)

    def emit(self, nc, block, sems):
        def val(k, s):
            m = self.marks.get(k, [])
            return bisect.bisect_right(m, s) * INC.get(k, 1)
        i = 0
        n = len(self.ops)
        seq = defaultdict(int)
        while i < n:
            if self.ops[i][0] == "BARRIER":
                i += 1
                continue
            phys = PHYS[self.ops[i][0]]
            j = i
            while j < n and self.ops[j][0] != "BARRIER" \
                    and PHYS[self.ops[j][0]] == phys:
                j += 1
            lo, hi = i, j

            def body(eng, lo=lo, hi=hi):
                for t in range(lo, hi):
                    key, emit, _, _, tag = self.ops[t]
                    s = seq[key]
                    seq[key] += 1
                    for k2, s2 in sorted(self.deps[t].items()):
                        eng.wait_ge(sems[k2], val(k2, s2))
                    instr = emit(eng)
                    if s in self.marks.get(key, []):
                        assert instr is not None, (key, s, tag)
                        instr.then_inc(sems[key], INC.get(key, 1))
            getattr(block, phys)(body)
            i = j
        def final(eng):
            for k in self.marks:
                m = self.marks[k]
                if m:
                    eng.wait_ge(sems[k], len(m) * INC.get(k, 1))
        block.sync(final)


def ceil_div(a, b):
    return (a + b - 1) // b


def build(debug=()):
    nc = bass.Bass("TRN2", target_bir_lowering=False)
    dt = nc.dram_tensor

    u = dt("u", [L, DM], F32, kind="ExternalInput").ap()
    winf = dt("winf", [DM, DIN], F32, kind="ExternalInput").ap()
    woutf = dt("woutf", [DI, DM], F32, kind="ExternalInput").ap()
    gpTf = dt("gpTf", [NE, DM, IE], F32, kind="ExternalInput").ap()
    upTf = dt("upTf", [NE, DM, IE], F32, kind="ExternalInput").ap()
    dpTf = dt("dpTf", [NE, IE, DM], F32, kind="ExternalInput").ap()
    shgTf = dt("shgTf", [DM, 2 * IE], F32, kind="ExternalInput").ap()
    shuTf = dt("shuTf", [DM, 2 * IE], F32, kind="ExternalInput").ap()
    shdTf = dt("shdTf", [2 * IE, DM], F32, kind="ExternalInput").ap()
    gwTf = dt("gwTf", [DM, NE], F32, kind="ExternalInput").ap()
    convw = dt("convw", [128, 136], F32, kind="ExternalInput").ap()
    convb = dt("convb", [128, 34], F32, kind="ExternalInput").ap()
    dtbbc = dt("dtbbc", [128, NH], F32, kind="ExternalInput").ap()
    alogbc = dt("alogbc", [128, NH], F32, kind="ExternalInput").ap()
    dbc = dt("dbc", [128, NH], F32, kind="ExternalInput").ap()
    normwbc = dt("normwbc", [128, DI], F32, kind="ExternalInput").ap()
    ebiasbc = dt("ebiasbc", [128, NE], F32, kind="ExternalInput").ap()
    efacbc = dt("efacbc", [128, NE], F32, kind="ExternalInput").ap()
    onessq = dt("onessq", [128, 128], F32, kind="ExternalInput").ap()
    tri = dt("tri", [128, 128], F32, kind="ExternalInput").ap()
    idb = dt("idb", [128, 128], BF16, kind="ExternalInput").ap()

    win_b = dt("win_b", [DM, DIN], BF16).ap()
    wout_b = dt("wout_b", [DI, DM], BF16).ap()
    gpT_b = dt("gpT_b", [NE, DM, IE], BF16).ap()
    upT_b = dt("upT_b", [NE, DM, IE], BF16).ap()
    dpT_b = dt("dpT_b", [NE, IE, DM], BF16).ap()
    shgT_b = dt("shgT_b", [DM, 2 * IE], BF16).ap()
    shuT_b = dt("shuT_b", [DM, 2 * IE], BF16).ap()
    shdT_b = dt("shdT_b", [2 * IE, DM], BF16).ap()
    xn_s = dt("xn_s", [16, 128, L], BF16).ap()
    xbc_s = dt("xbc_s", [34, 128, L], BF16).ap()
    siluz_s = dt("siluz_s", [L, DI], BF16).ap()
    tfm_s = dt("tfm_s", [NCH, 128, DI], BF16).ap()

    out = dt("out", [L, DM], F32, kind="ExternalOutput").ap()
    dbg = {}
    if debug:
        for name, shape, dty in [
            ("d_xn", [16, 128, L], BF16), ("d_xbc", [34, 128, L], BF16),
            ("d_siluz", [L, DI], BF16), ("d_dt", [128, 2048], F32),
            ("d_tfm", [NCH, 128, DI], BF16),
            ("d_xnbf", [L, DM], BF16), ("d_lnc", [L // 128, 128, 8], F32),
            ("d_dcol", [128, 6 * NH], F32), ("d_cumcl", [128, 128], F32),
            ("d_ecol", [128, 3 * NH], F32), ("d_xtm", [128, DI], BF16),
            ("d_cbt", [128, 128], F32), ("d_mbf", [128, NH * 128], BF16),
            ("d_yoff", [128, DI], F32), ("d_y", [128, DI], F32),
            ("d_xpre", [128, 520], BF16), ("d_xcv", [128, 34, 512], BF16),
            ("d_sco", [128, 4, 16], F32), ("d_comb", [128, 4, 16], F32),
            ("d_accw", [128, 4, 2048], F32), ("d_acc0", [128, 4, 2048], F32),
            ("d_int0", [128, 11, 512], BF16), ("d_tfmv", [128, 32, 512], BF16),
            ("d_cw", [128, 136], F32), ("d_cb", [128, 34], F32),
            ("d_cacc", [128, 512], F32),
            ("d_prev", [128, DI], F32), ("d_tbf", [128, DI], BF16),
        ]:
            if name in debug:
                dbg[name] = dt(name, shape, dty, kind="ExternalOutput").ap()

    sch = Sch()
    es = ExitStack()

    def sb(name, shape, dty=F32):
        return es.enter_context(nc.sbuf_tensor(name, shape, dty))

    c_tri = sb("c_tri", [128, 128])
    c_ones = sb("c_ones", [128, 128])
    c_idb = sb("c_idb", [128, 128], BF16)
    c_dtb = sb("c_dtb", [128, NH])
    c_nA = sb("c_nA", [128, NH])
    c_dbc = sb("c_dbc", [128, NH])
    c_convw = sb("c_convw", [128, 136])
    c_convb = sb("c_convb", [128, 34])
    dt_tm = sb("dt_tm", [128, 2048])

    PS = es.enter_context(nc.psum_tensor("PS", [128, 3584], F32))
    PT = es.enter_context(nc.psum_tensor("PT", [128, 1024], BF16))
    PSB = lambda b: PS[:, 512 * b:512 * (b + 1)]

    def cast_dma(dst, src, n_pieces, res, tag):
        rows = src.shape[0]
        step = ceil_div(rows, n_pieces)
        for r0 in range(0, rows, step):
            r1 = min(rows, r0 + step)
            def e(eng, dst=dst[r0:r1], src=src[r0:r1]):
                return eng.dma_start(out=dst, in_=src)
            sch.add("gd", e, reads=(), writes=(res,), tag=tag)

    cast_dma(win_b, winf, 4, "win_b", "cast_win")
    cast_dma(wout_b, woutf, 2, "wout_b", "cast_wout")
    cast_dma(shgT_b, shgTf, 2, "shgT_b", "cast_shg")
    cast_dma(shuT_b, shuTf, 2, "shuT_b", "cast_shu")
    cast_dma(shdT_b, shdTf, 2, "shdT_b", "cast_shd")
    for e_ in range(NE):
        cast_dma(gpT_b[e_], gpTf[e_], 1, "gpT_b", "cast_gp")
        cast_dma(upT_b[e_], upTf[e_], 1, "upT_b", "cast_up")
        cast_dma(dpT_b[e_], dpTf[e_], 1, "dpT_b", "cast_dp")

    for tile, src, res in [(c_tri, tri, "c_tri"), (c_ones, onessq, "c_ones"),
                           (c_idb, idb, "c_idb"), (c_dtb, dtbbc, "c_dtb"),
                           (c_dbc, dbc, "c_dbc"), (c_convw, convw, "c_convw"),
                           (c_convb, convb, "c_convb")]:
        def e(eng, tile=tile, src=src):
            return eng.dma_start(out=tile[:], in_=src)
        sch.add("sp", e, writes=(res,), tag="const")
    c_alog = sb("c_alog", [128, NH])
    sch.add("sp", lambda eng: eng.dma_start(out=c_alog[:], in_=alogbc),
            writes=("c_alog",), tag="const")
    sch.add("act", lambda eng: eng.activation(c_nA[:], c_alog[:], AF.Exp),
            reads=("c_alog",), writes=("c_nA",), tag="negA")
    sch.add("dve", lambda eng: eng.tensor_scalar_mul(c_nA[:], c_nA[:], -1.0),
            reads=("c_nA",), writes=("c_nA",), tag="negA2")

    with ExitStack() as s1es:
        def sb1(name, shape, dty=F32):
            return s1es.enter_context(nc.sbuf_tensor(name, shape, dty))
        u_t = [sb1(f"u_t{i}", [128, DM]) for i in range(2)]
        ln_sq = sb1("ln_sq", [128, DM])
        ln_c = sb1("ln_c", [128, 8])
        xn_bf = sb1("xn_bf", [128, DM], BF16)
        xn_fm = sb1("xn_fm", [128, 16, TT1], BF16)
        win_t = [sb1(f"win_t{i}", [128, 16, 512], BF16) for i in range(2)]
        zst = [sb1(f"zst{i}", [128, 512], BF16) for i in range(2)]
        xbst = [sb1(f"xbst{i}", [128, 512], BF16) for i in range(2)]

        for tile in range(NT1):
            t0 = tile * TT1
            for g in range(8):
                sl = g % 2
                us = f"u{sl}"
                def e(eng, t0=t0, g=g, sl=sl):
                    return eng.dma_start(out=u_t[sl][:],
                                         in_=u[t0 + g * 128:t0 + (g + 1) * 128, :])
                sch.add("sp", e, writes=(us,), tag=f"uload{tile}.{g}")

                def e(eng, sl=sl):
                    return eng.tensor_reduce(ln_c[:, 0:1], u_t[sl][:], AX.X, ALU.add)
                sch.add("dve", e, reads=(us,), writes=("lnc",), tag="lnsum")

                def e(eng, sl=sl):
                    return eng.activation(ln_sq[:], u_t[sl][:], AF.Square)
                sch.add("act", e, reads=(us,), writes=("lnsq",), tag="lnsq")

                def e(eng):
                    return eng.tensor_reduce(ln_c[:, 1:2], ln_sq[:], AX.X, ALU.add)
                sch.add("dve", e, reads=("lnsq",), writes=("lnc",), tag="lnssq")

                def e(eng):
                    return eng.tensor_scalar_mul(ln_c[:, 2:3], ln_c[:, 0:1], 1.0 / DM)
                sch.add("dve", e, reads=("lnc",), writes=("lnc",), tag="mean")

                def e(eng):
                    return eng.tensor_tensor(ln_c[:, 3:4], ln_c[:, 2:3],
                                             ln_c[:, 2:3], ALU.mult)
                sch.add("dve", e, reads=("lnc",), writes=("lnc",), tag="msq")

                def e(eng):
                    return eng.scalar_tensor_tensor(ln_c[:, 4:5], ln_c[:, 1:2],
                                                    1.0 / DM, ln_c[:, 3:4],
                                                    ALU.mult, ALU.subtract)
                sch.add("dve", e, reads=("lnc",), writes=("lnc",), tag="var")

                def e(eng):
                    return eng.tensor_scalar_add(ln_c[:, 5:6], ln_c[:, 4:5], EPS)
                sch.add("dve", e, reads=("lnc",), writes=("lnc",), tag="vareps")

                def e(eng):
                    return eng.activation(ln_c[:, 6:7], ln_c[:, 5:6], AF.Ln)
                sch.add("act", e, reads=("lnc",), writes=("lnc",), tag="lnv")

                def e(eng):
                    return eng.tensor_scalar_mul(ln_c[:, 5:6], ln_c[:, 6:7], -0.5)
                sch.add("dve", e, reads=("lnc",), writes=("lnc",), tag="halfln")

                def e(eng):
                    return eng.activation(ln_c[:, 7:8], ln_c[:, 5:6], AF.Exp)
                sch.add("act", e, reads=("lnc",), writes=("lnc",), tag="rstd")

                def e(eng, sl=sl):
                    return eng.tensor_scalar(xn_bf[:], u_t[sl][:], ln_c[:, 2:3],
                                             ln_c[:, 7:8], ALU.subtract, ALU.mult)
                sch.add("dve", e, reads=(us, "lnc"), writes=("xnbf",), tag="xn")

                if "d_xnbf" in dbg:
                    def e(eng, t0=t0, g=g):
                        return eng.dma_start(
                            out=dbg["d_xnbf"][t0 + 128 * g:t0 + 128 * (g + 1), :],
                            in_=xn_bf[:])
                    sch.add("gd", e, reads=("xnbf",), writes=("dxnbf",), tag="dxnbf")
                if "d_lnc" in dbg:
                    def e(eng, t0=t0, g=g, tile=tile):
                        return eng.dma_start(
                            out=dbg["d_lnc"][tile * 8 + g], in_=ln_c[:])
                    sch.add("gd", e, reads=("lnc",), writes=("dlnc",), tag="dlnc")
                for r in range(2):
                    def e(eng, r=r):
                        last = None
                        for k in range(8):
                            hc = 8 * r + k
                            last = eng.transpose(
                                PT[:, 128 * k:128 * (k + 1)],
                                xn_bf[:, 128 * hc:128 * (hc + 1)], c_idb[:])
                        return last
                    sch.add("pe", e, reads=("xnbf", "c_idb"), writes=("psb7",),
                            tag=f"xnT{g}.{r}")

                    def e(eng, r=r, g=g):
                        return eng.copy(
                            xn_fm[:, 8 * r:8 * (r + 1), 128 * g:128 * (g + 1)],
                            PT[:].rearrange("p (a b) -> p a b", a=8))
                    sch.add("act", e, reads=("psb7",), writes=("xnfm",),
                            tag=f"xnTe{g}.{r}")

            for hc in range(16):
                def e(eng, hc=hc, t0=t0):
                    return eng.dma_start(out=xn_s[hc][:, t0:t0 + TT1],
                                         in_=xn_fm[:, hc, :])
                sch.add("gd", e, reads=("xnfm",), writes=("xn_s",), tag="xnstore")

            for cb in range(17):
                wsl = cb % 2
                wres = f"win{wsl}"
                ncols = 512 if cb < 16 else 320
                c0 = cb * 512
                for half in range(2):
                    def e(eng, wsl=wsl, c0=c0, ncols=ncols, half=half):
                        h0 = half * 8
                        return eng.dma_start(
                            out=win_t[wsl][:, h0:h0 + 8, 0:ncols],
                            in_=win_b.rearrange("(hc p) c -> p hc c", p=128)
                            [:, h0:h0 + 8, c0:c0 + ncols])
                    sch.add("sp", e, reads=("win_b",), writes=(wres,),
                            tag=f"winld{tile}.{cb}.{half}")

                if cb < 8:
                    for g in range(8):
                        bk = (cb * 8 + g) % 6
                        def e(eng, g=g, bk=bk, wsl=wsl):
                            last = None
                            for hc in range(16):
                                last = eng.matmul(
                                    PSB(bk),
                                    xn_fm[:, hc, 128 * g:128 * (g + 1)],
                                    win_t[wsl][:, hc, :],
                                    start=(hc == 0), stop=(hc == 15))
                            return last
                        sch.add("pe", e, reads=("xnfm", wres),
                                writes=(f"psb{bk}",), tag=f"zmm{cb}.{g}")

                        zsl = g % 2
                        def e(eng, bk=bk, zsl=zsl):
                            return eng.activation(zst[zsl][:], PSB(bk), AF.Silu)
                        sch.add("act", e, reads=(f"psb{bk}",),
                                writes=(f"zst{zsl}",), tag=f"zev{cb}.{g}")

                        def e(eng, t0=t0, g=g, cb=cb, zsl=zsl):
                            return eng.dma_start(
                                out=siluz_s[t0 + 128 * g:t0 + 128 * (g + 1),
                                            512 * cb:512 * (cb + 1)],
                                in_=zst[zsl][:])
                        sch.add("gd", e, reads=(f"zst{zsl}",),
                                writes=("siluz_s",), tag=f"zst{cb}.{g}")
                else:
                    nsub = 4 if cb < 16 else 2
                    for m in range(nsub):
                        ch = (cb - 8) * 4 + m
                        for half in range(2):
                            bk = (ch * 2 + half) % 6
                            def e(eng, m=m, half=half, bk=bk, wsl=wsl):
                                last = None
                                for hc in range(16):
                                    last = eng.matmul(
                                        PSB(bk),
                                        win_t[wsl][:, hc, 128 * m:128 * (m + 1)],
                                        xn_fm[:, hc, 512 * half:512 * (half + 1)],
                                        start=(hc == 0), stop=(hc == 15))
                                return last
                            sch.add("pe", e, reads=("xnfm", wres),
                                    writes=(f"psb{bk}",), tag=f"xmm{cb}.{m}.{half}")

                            xsl = half
                            eng_key = "act" if half == 0 else "dve"
                            def e(eng, bk=bk, xsl=xsl):
                                return eng.tensor_copy(xbst[xsl][:], PSB(bk)) \
                                    if not hasattr(eng, "activation") else \
                                    eng.activation(xbst[xsl][:], PSB(bk), AF.Copy)
                            def e(eng, bk=bk, xsl=xsl):
                                return eng.tensor_copy(xbst[xsl][:], PSB(bk))
                            sch.add("dve", e, reads=(f"psb{bk}",),
                                    writes=(f"xbst{xsl}",), tag=f"xev{cb}.{m}.{half}")

                            def e(eng, ch=ch, half=half, t0=t0, xsl=xsl):
                                return eng.dma_start(
                                    out=xbc_s[ch][:, t0 + 512 * half:
                                                  t0 + 512 * (half + 1)],
                                    in_=xbst[xsl][:])
                            sch.add("gd", e, reads=(f"xbst{xsl}",),
                                    writes=("xbc_s",), tag=f"xst{cb}.{m}.{half}")
                    if cb == 16:
                        for g in range(8):
                            def e(eng, g=g, wsl=wsl):
                                last = None
                                for hc in range(16):
                                    last = eng.matmul(
                                        PS[:, 6 * 512 + 64 * g:6 * 512 + 64 * (g + 1)],
                                        xn_fm[:, hc, 128 * g:128 * (g + 1)],
                                        win_t[wsl][:, hc, 256:320],
                                        start=(hc == 0), stop=(hc == 15))
                                return last
                            sch.add("pe", e, reads=("xnfm", wres),
                                    writes=("psb6",), tag=f"dtmm{tile}.{g}")

                        def e(eng, tile=tile):
                            return eng.tensor_copy(
                                dt_tm[:, 512 * tile:512 * (tile + 1)],
                                PS[:, 6 * 512:7 * 512])
                        sch.add("dve", e, reads=("psb6",), writes=("dttm",),
                                tag=f"dtev{tile}")

    sch.barrier()

    with ExitStack() as s2es:
        def sb2(name, shape, dty=F32):
            return s2es.enter_context(nc.sbuf_tensor(name, shape, dty))
        xbc_t = sb2("xbc_t", [128, 34, 520], BF16)
        cacc = [sb2(f"cacc{i}", [128, 512]) for i in range(2)]
        x_tm = sb2("x_tm", [128, DI], BF16)
        b_tm = sb2("b_tm", [128, 128], BF16)
        sszt = sb2("sszt", [128, DI], BF16)
        dcol = sb2("dcol", [128, 6 * NH])
        cumcl = sb2("cumcl", [128, 128])
        ecol = sb2("ecol", [128, 3 * NH])
        rcol = sb2("rcol", [128, 4])
        XW = sb2("XW", [128, DI], BF16)
        XW2 = sb2("XW2", [128, DI], BF16)
        scrA = sb2("scrA", [128, DI])
        scrB = sb2("scrB", [128, DI])
        Mbf = sb2("Mbf", [128, NH, 128], BF16)
        prevA = sb2("prevA", [128, DI])
        prevB = sb2("prevB", [128, DI])
        prevbf = sb2("prevbf", [128, DI], BF16)
        CBt = sb2("CBt", [128, 128])
        tbf = sb2("tbf", [128, DI], BF16)
        tfmc = sb2("tfmc", [128, 32, 128], BF16)
        c_normw = sb2("c_normw", [128, DI])

        sch.add("sp", lambda eng: eng.dma_start(out=c_normw[:], in_=normwbc),
                writes=("c_normw",), tag="normw")
        sch.add("dve", lambda eng: eng.memset(prevA[:], 0.0),
                writes=("prevA",), tag="previnit")

        def D(key, emit, reads=(), writes=(), tag=""):
            sch.add(key, emit, reads, writes, tag)

        def snap(name, ap, res):
            if name in dbg:
                def e(eng, name=name, ap=ap):
                    return eng.dma_start(out=dbg[name], in_=ap)
                sch.add("gd", e, reads=(res,), writes=(), tag=f"snap_{name}")

        dc = lambda i: dcol[:, NH * i:NH * (i + 1)]
        ec = lambda i: ecol[:, NH * i:NH * (i + 1)]

        for st5 in range(NT5):
            t0 = st5 * TT5
            for ch in range(34):
                res = f"xbct{ch}"
                if t0 == 0:
                    def e(eng, ch=ch):
                        return eng.dma_start(out=xbc_t[:, ch, 4:516],
                                             in_=xbc_s[ch][:, 0:512])
                    D("sp", e, reads=("xbc_s",), writes=(res,), tag=f"xld{st5}.{ch}")
                    def e(eng, ch=ch):
                        return eng.memset(xbc_t[:, ch, 0:4], 0.0)
                    D("dve", e, writes=(res,), tag=f"xpad{ch}")
                else:
                    def e(eng, ch=ch, t0=t0):
                        return eng.dma_start(out=xbc_t[:, ch, 0:516],
                                             in_=xbc_s[ch][:, t0 - 4:t0 + 512])
                    D("sp", e, reads=("xbc_s",), writes=(res,), tag=f"xld{st5}.{ch}")
                if st5 == 0 and ch == 0:
                    snap("d_xpre", xbc_t[:, 0, :], "xbct0")
                ca, cbk = f"cacc{0}", f"cacc{1}"
                def e(eng, ch=ch):
                    return eng.tensor_scalar(cacc[0][:], xbc_t[:, ch, 1:513],
                                             c_convw[:, 4 * ch:4 * ch + 1], None,
                                             ALU.mult)
                D("dve", e, reads=(res, "c_convw"), writes=(ca,), tag=f"cv0.{ch}")
                for k in (1, 2):
                    def e(eng, ch=ch, k=k):
                        return eng.scalar_tensor_tensor(
                            cacc[k % 2][:], xbc_t[:, ch, 1 + k:1 + k + 512],
                            c_convw[:, 4 * ch + k:4 * ch + k + 1],
                            cacc[(k + 1) % 2][:], ALU.mult, ALU.add)
                    D("dve", e, reads=(res, "c_convw", (ca if k % 2 == 1 else cbk)),
                      writes=((cbk if k % 2 == 1 else ca),), tag=f"cv{k}.{ch}")
                def e(eng, ch=ch):
                    return eng.scalar_tensor_tensor(
                        cacc[1][:], xbc_t[:, ch, 4:516],
                        c_convw[:, 4 * ch + 3:4 * ch + 4],
                        cacc[0][:], ALU.mult, ALU.add)
                D("dve", e, reads=(res, "c_convw", ca), writes=(cbk,), tag=f"cv3.{ch}")
                if st5 == 0 and ch == 0:
                    snap("d_cacc", cacc[1][:], "cacc1")
                    snap("d_cw", c_convw[:], "c_convw")
                    snap("d_cb", c_convb[:], "c_convb")
                def e(eng, ch=ch):
                    return eng.activation(xbc_t[:, ch, 4:516], cacc[1][:], AF.Silu,
                                          bias=c_convb[:, ch:ch + 1])
                D("act", e, reads=(cbk, "c_convb", res), writes=(res,),
                  tag=f"cvs.{ch}")

            if st5 == 0 and "d_xcv" in dbg:
                for ch2 in range(34):
                    def e(eng, ch2=ch2):
                        return eng.dma_start(out=dbg["d_xcv"][:, ch2, :],
                                             in_=xbc_t[:, ch2, 4:516])
                    sch.add("gd", e, reads=(f"xbct{ch2}",), writes=(), tag="dxcv")
            for loc in range(4):
                c = st5 * 4 + loc
                col0 = 4 + loc * 128

                def e(eng, c=c):
                    return eng.dma_start(out=sszt[:],
                                         in_=siluz_s[128 * c:128 * (c + 1), :])
                D("sp", e, reads=("siluz_s",), writes=("sszt",), tag=f"szld{c}")

                def e(eng, c=c):
                    return eng.tensor_add(dc(0), dt_tm[:, NH * c:NH * (c + 1)],
                                          c_dtb[:])
                D("dve", e, reads=("dttm", "c_dtb"), writes=("dcol",), tag="sp_v")
                D("act", lambda eng: eng.activation(dc(1), dc(0), AF.Abs),
                  reads=("dcol",), writes=("dcol",), tag="sp_abs")
                D("act", lambda eng: eng.activation(dc(2), dc(1), AF.Exp, scale=-1.0),
                  reads=("dcol",), writes=("dcol",), tag="sp_exp")
                D("act", lambda eng: eng.activation(dc(1), dc(2), AF.Ln, bias=1.0),
                  reads=("dcol",), writes=("dcol",), tag="sp_ln")
                D("dve", lambda eng: eng.tensor_scalar_max(dc(2), dc(0), 0.0),
                  reads=("dcol",), writes=("dcol",), tag="sp_max")
                D("dve", lambda eng: eng.tensor_add(dc(4), dc(1), dc(2)),
                  reads=("dcol",), writes=("dcol",), tag="sp_dtf")
                D("dve", lambda eng: eng.tensor_mul(dc(5), dc(4), c_nA[:]),
                  reads=("dcol", "c_nA"), writes=("dcol",), tag="sp_da")

                if c == 0:
                    snap("d_dcol", dcol[:], "dcol")
                def e(eng):
                    eng.matmul(PS[:, 0:64], c_tri[:], dc(5), start=True, stop=True)
                    return eng.matmul(PS[:, 64:128], c_ones[:], dc(5),
                                      start=True, stop=True)
                D("pe", e, reads=("dcol", "c_tri", "c_ones"), writes=("psb0",),
                  tag=f"cumm{c}")
                D("dve", lambda eng: eng.tensor_copy(cumcl[:], PS[:, 0:128]),
                  reads=("psb0",), writes=("cumcl",), tag="cumev")
                def e(eng):
                    return eng.tensor_sub(ec(0), cumcl[:, 64:128], cumcl[:, 0:64])
                D("dve", e, reads=("cumcl",), writes=("ecol",), tag="dl0")
                D("act", lambda eng: eng.activation(ec(0), ec(0), AF.Exp),
                  reads=("ecol",), writes=("ecol",), tag="dlast")
                def e(eng):
                    return eng.activation(ec(1), cumcl[:, 64:128], AF.Exp)
                D("act", e, reads=("cumcl",), writes=("ecol",), tag="atot")
                def e(eng):
                    return eng.activation(ec(2), cumcl[:, 0:64], AF.Exp)
                D("act", e, reads=("cumcl",), writes=("ecol",), tag="expcum")

                if c == 0:
                    snap("d_cumcl", cumcl[:], "cumcl")
                    snap("d_ecol", ecol[:], "ecol")
                for r in range(4):
                    def e(eng, r=r, col0=col0):
                        last = None
                        for k in range(8):
                            cc = 8 * r + k
                            last = eng.transpose(PT[:, 128 * k:128 * (k + 1)],
                                                 xbc_t[:, cc, col0:col0 + 128],
                                                 c_idb[:])
                        return last
                    D("pe", e,
                      reads=tuple(f"xbct{8 * r + k}" for k in range(8)) + ("c_idb",),
                      writes=("psb7",), tag=f"xT{c}.{r}")
                    def e(eng, r=r):
                        return eng.copy(
                            x_tm[:, 1024 * r:1024 * (r + 1)], PT[:])
                    D("act", e, reads=("psb7",), writes=("x_tm",), tag=f"xTe{c}.{r}")
                def e(eng, col0=col0):
                    return eng.transpose(PT[:, 0:128],
                                         xbc_t[:, 32, col0:col0 + 128], c_idb[:])
                D("pe", e, reads=("xbct32", "c_idb"), writes=("psb7",), tag=f"bT{c}")
                D("act", lambda eng: eng.copy(b_tm[:], PT[:, 0:128]),
                  reads=("psb7",), writes=("b_tm",), tag=f"bTe{c}")

                def e(eng):
                    return eng.tensor_tensor(
                        XW[:].rearrange("p (h q) -> p h q", h=NH),
                        x_tm[:].rearrange("p (h q) -> p h q", h=NH),
                        dc(4).unsqueeze(2).broadcast_to([128, NH, HD]), ALU.mult)
                D("dve", e, reads=("x_tm", "dcol"), writes=("XW",), tag=f"xw{c}")
                def e(eng):
                    return eng.tensor_tensor(
                        XW2[:].rearrange("p (h q) -> p h q", h=NH),
                        XW[:].rearrange("p (h q) -> p h q", h=NH),
                        ec(0).unsqueeze(2).broadcast_to([128, NH, HD]), ALU.mult)
                D("dve", e, reads=("XW", "ecol"), writes=("XW2",), tag=f"xw2{c}")

                if c == 0:
                    snap("d_xtm", x_tm[:], "x_tm")
                def e(eng, col0=col0):
                    return eng.matmul(PS[:, 512:640], xbc_t[:, 32, col0:col0 + 128],
                                      xbc_t[:, 33, col0:col0 + 128],
                                      start=True, stop=True)
                D("pe", e, reads=("xbct32", "xbct33"), writes=("psb1",),
                  tag=f"cbmm{c}")
                D("dve", lambda eng: eng.tensor_copy(CBt[:], PS[:, 512:640]),
                  reads=("psb1",), writes=("CBt",), tag="cbev")

                for q in range(4):
                    h0 = 16 * q
                    def e(eng, h0=h0):
                        return eng.affine_select(
                            scrA[:, 0:2048].rearrange("p (h i) -> p h i", h=16),
                            dc(5)[:, h0:h0 + 16].unsqueeze(2)
                            .broadcast_to([128, 16, 128]),
                            [[0, 16], [1, 128]], ALU.is_ge, 0.0,
                            channel_multiplier=-1)
                    D("pool", e, reads=("dcol",), writes=("scrA",), tag=f"rmask{c}.{q}")
                    def e(eng):
                        last = None
                        for n in range(4):
                            last = eng.matmul(PSB(n), c_ones[:],
                                              scrA[:, 512 * n:512 * (n + 1)],
                                              start=True, stop=True)
                        return last
                    D("pe", e, reads=("scrA", "c_ones"),
                      writes=("psb0", "psb1", "psb2", "psb3"), tag=f"cbc{c}.{q}")
                    def e(eng, h0=h0):
                        return eng.tensor_sub(
                            scrB[:, 0:2048].rearrange("p (h i) -> p h i", h=16),
                            PS[:, 0:2048].rearrange("p (h i) -> p h i", h=16),
                            cumcl[:, h0:h0 + 16].unsqueeze(2)
                            .broadcast_to([128, 16, 128]))
                    D("dve", e, reads=("psb0", "psb1", "psb2", "psb3", "cumcl"),
                      writes=("scrB",), tag=f"seg{c}.{q}")
                    D("act", lambda eng: eng.activation(scrA[:, 0:2048],
                                                        scrB[:, 0:2048], AF.Exp),
                      reads=("scrB",), writes=("scrA",), tag=f"dexp{c}.{q}")
                    def e(eng):
                        return eng.affine_select(
                            scrB[:, 0:2048].rearrange("p (h i) -> p h i", h=16),
                            scrA[:, 0:2048].rearrange("p (h i) -> p h i", h=16),
                            [[0, 16], [1, 128]], ALU.is_ge, 0.0,
                            channel_multiplier=-1)
                    D("pool", e, reads=("scrA",), writes=("scrB",), tag=f"dmask{c}.{q}")
                    def e(eng):
                        return eng.tensor_tensor(
                            scrA[:, 0:2048].rearrange("p (h i) -> p h i", h=16),
                            scrB[:, 0:2048].rearrange("p (h i) -> p h i", h=16),
                            CBt[:].unsqueeze(1).broadcast_to([128, 16, 128]),
                            ALU.mult)
                    D("dve", e, reads=("scrB", "CBt"), writes=("scrA",),
                      tag=f"mcb{c}.{q}")
                    def e(eng, h0=h0):
                        return eng.tensor_copy(
                            Mbf[:, h0:h0 + 16, :],
                            scrA[:, 0:2048].rearrange("p (h i) -> p h i", h=16))
                    D("dve", e, reads=("scrA",), writes=("Mbf",),
                      tag=f"mcast{c}.{q}")

                if c == 0:
                    snap("d_cbt", CBt[:], "CBt")
                    snap("d_mbf", Mbf[:].rearrange("p a b -> p (a b)"), "Mbf")
                D("dve", lambda eng: eng.tensor_copy(prevbf[:], prevA[:]),
                  reads=("prevA",), writes=("prevbf",), tag=f"pbf{c}")
                def e(eng):
                    return eng.tensor_tensor(
                        prevB[:].rearrange("p (h q) -> p h q", h=NH),
                        prevA[:].rearrange("p (h q) -> p h q", h=NH),
                        ec(1).unsqueeze(2).broadcast_to([128, NH, HD]), ALU.mult)
                D("dve", e, reads=("prevA", "ecol"), writes=("prevB",), tag=f"pB{c}")

                for hh in range(2):
                    o0 = 2048 * hh
                    h0 = 32 * hh
                    def e(eng, o0=o0, col0=col0):
                        last = None
                        for n in range(4):
                            last = eng.matmul(PSB(n),
                                              xbc_t[:, 33, col0:col0 + 128],
                                              prevbf[:, o0 + 512 * n:o0 + 512 * (n + 1)],
                                              start=True, stop=True)
                        return last
                    D("pe", e, reads=("xbct33", "prevbf"),
                      writes=("psb0", "psb1", "psb2", "psb3"), tag=f"yoff{c}.{hh}")
                    def e(eng, o0=o0, h0=h0):
                        return eng.tensor_tensor(
                            scrB[:, o0:o0 + 2048].rearrange("p (h q) -> p h q", h=32),
                            PS[:, 0:2048].rearrange("p (h q) -> p h q", h=32),
                            ec(2)[:, h0:h0 + 32].unsqueeze(2)
                            .broadcast_to([128, 32, HD]), ALU.mult)
                    D("dve", e, reads=("psb0", "psb1", "psb2", "psb3", "ecol"),
                      writes=("scrB",), tag=f"yoffev{c}.{hh}")
                    def e(eng, o0=o0):
                        last = None
                        for n in range(4):
                            last = eng.matmul(PSB(n), b_tm[:],
                                              XW2[:, o0 + 512 * n:o0 + 512 * (n + 1)],
                                              start=True, stop=True)
                        return last
                    D("pe", e, reads=("b_tm", "XW2"),
                      writes=("psb0", "psb1", "psb2", "psb3"), tag=f"stat{c}.{hh}")
                    def e(eng, o0=o0):
                        return eng.tensor_add(prevA[:, o0:o0 + 2048],
                                              prevB[:, o0:o0 + 2048],
                                              PS[:, 0:2048])
                    D("dve", e, reads=("prevB", "psb0", "psb1", "psb2", "psb3"),
                      writes=("prevA",), tag=f"pup{c}.{hh}")
                    def e(eng, h0=h0):
                        last = None
                        for k in range(32):
                            h = h0 + k
                            last = eng.matmul(
                                PS[:, 64 * k:64 * (k + 1)],
                                Mbf[:, h, :], XW[:, 64 * h:64 * (h + 1)],
                                start=True, stop=True)
                        return last
                    D("pe", e, reads=("Mbf", "XW"),
                      writes=("psb0", "psb1", "psb2", "psb3"), tag=f"ydiag{c}.{hh}")
                    def e(eng, o0=o0, h0=h0):
                        return eng.tensor_tensor(
                            scrA[:, o0:o0 + 2048].rearrange("p (h q) -> p h q", h=32),
                            x_tm[:, o0:o0 + 2048].rearrange("p (h q) -> p h q", h=32),
                            c_dbc[:, h0:h0 + 32].unsqueeze(2)
                            .broadcast_to([128, 32, HD]), ALU.mult)
                    D("dve", e, reads=("x_tm", "c_dbc"), writes=("scrA",),
                      tag=f"yxd{c}.{hh}")
                    def e(eng, o0=o0):
                        return eng.tensor_add(scrA[:, o0:o0 + 2048],
                                              scrA[:, o0:o0 + 2048],
                                              scrB[:, o0:o0 + 2048])
                    D("dve", e, reads=("scrA", "scrB"), writes=("scrA",),
                      tag=f"yadd{c}.{hh}")
                    def e(eng, o0=o0):
                        return eng.tensor_add(scrA[:, o0:o0 + 2048],
                                              scrA[:, o0:o0 + 2048],
                                              PS[:, 0:2048])
                    D("dve", e, reads=("scrA", "psb0", "psb1", "psb2", "psb3"),
                      writes=("scrA",), tag=f"ydg{c}.{hh}")

                if c == 0:
                    snap("d_yoff", scrB[:], "scrB")
                    snap("d_y", scrA[:], "scrA")
                    snap("d_prev", prevA[:], "prevA")
                def e(eng):
                    return eng.tensor_mul(scrA[:], scrA[:], sszt[:])
                D("dve", e, reads=("scrA", "sszt"), writes=("scrA",), tag=f"tz{c}")
                def e(eng):
                    return eng.activation(scrB[:], scrA[:], AF.Square)
                D("act", e, reads=("scrA",), writes=("scrB",), tag=f"rms{c}")
                def e(eng):
                    return eng.tensor_reduce(rcol[:, 0:1], scrB[:], AX.X, ALU.add)
                D("dve", e, reads=("scrB",), writes=("rcol",), tag=f"rmsr{c}")
                def e(eng):
                    return eng.tensor_scalar(rcol[:, 1:2], rcol[:, 0:1],
                                             1.0 / DI, EPS, ALU.mult, ALU.add)
                D("dve", e, reads=("rcol",), writes=("rcol",), tag="rms2")
                D("act", lambda eng: eng.activation(rcol[:, 2:3], rcol[:, 1:2],
                                                    AF.Ln),
                  reads=("rcol",), writes=("rcol",), tag="rmsln")
                D("dve", lambda eng: eng.tensor_scalar_mul(rcol[:, 1:2],
                                                           rcol[:, 2:3], -0.5),
                  reads=("rcol",), writes=("rcol",), tag="rmsh")
                D("act", lambda eng: eng.activation(rcol[:, 3:4], rcol[:, 1:2],
                                                    AF.Exp),
                  reads=("rcol",), writes=("rcol",), tag="rstd")
                def e(eng):
                    return eng.tensor_scalar(scrA[:], scrA[:], rcol[:, 3:4], None,
                                             ALU.mult)
                D("dve", e, reads=("scrA", "rcol"), writes=("scrA",), tag=f"trs{c}")
                def e(eng):
                    return eng.tensor_mul(tbf[:], scrA[:], c_normw[:])
                D("dve", e, reads=("scrA", "c_normw"), writes=("tbf",), tag=f"tnw{c}")

                if c == 0:
                    snap("d_tbf", tbf[:], "tbf")
                for r in range(4):
                    def e(eng, r=r):
                        last = None
                        for k in range(8):
                            cc = 8 * r + k
                            last = eng.transpose(PT[:, 128 * k:128 * (k + 1)],
                                                 tbf[:, 128 * cc:128 * (cc + 1)],
                                                 c_idb[:])
                        return last
                    D("pe", e, reads=("tbf", "c_idb"), writes=("psb7",),
                      tag=f"tT{c}.{r}")
                    def e(eng, r=r):
                        return eng.copy(tfmc[:, 8 * r:8 * (r + 1), :],
                                        PT[:].rearrange("p (a b) -> p a b", a=8))
                    D("act", e, reads=("psb7",), writes=("tfmc",), tag=f"tTe{c}.{r}")
                def e(eng, c=c):
                    return eng.dma_start(out=tfm_s[c], in_=tfmc[:])
                D("gd", e, reads=("tfmc",), writes=("tfm_s",), tag=f"tst{c}")

    sch.barrier()

    with ExitStack() as s5es:
        def sb5(name, shape, dty=F32):
            return s5es.enter_context(nc.sbuf_tensor(name, shape, dty))
        xn5 = sb5("xn5", [128, 16, TT5], BF16)
        AR1 = sb5("AR1", [128, 16, IE], BF16)
        AR2 = sb5("AR2", [128, 16, IE], BF16)
        BRu = sb5("BRu", [128, 11 * DM], BF16)
        ACC = sb5("ACC", [128, 4, DM])
        inter = sb5("inter", [128, 11, TT5], BF16)
        sgt = sb5("sgt", [128, TT5])
        gwT = sb5("gwT", [128, 16, NE], BF16)
        sco = sb5("sco", [128, 4, NE])
        gat = sb5("gat", [128, 4, 24])
        wrk = sb5("wrk", [128, 4, NE])
        sel = sb5("sel", [128, 4, NE])
        comb = sb5("comb", [128, 4, NE])
        m8g = sb5("m8g", [128, 4, 8])
        dpT_v = BRu[:].rearrange("p (a b) -> p a b", a=11)
        tfm_v = AR1[:].rearrange("p a b -> p (a b)")[:, 0:16384] \
            .rearrange("p (a b) -> p a b", a=32)
        wo_v = AR2[:].rearrange("p a b -> p (a b)")[:, 0:16384] \
            .rearrange("p (a b) -> p a b", a=8)

        c_ebias = sb5("c_ebias", [128, NE])
        c_efac = sb5("c_efac", [128, NE])

        def e(eng):
            return eng.dma_start(out=gwT[:],
                                 in_=gwTf.rearrange("(hc p) e -> p hc e", p=128))
        sch.add("gd", e, writes=("gwT",), tag="gwT")
        sch.add("sp", lambda eng: eng.dma_start(out=c_ebias[:], in_=ebiasbc),
                writes=("c_ebias",), tag="ebias")
        sch.add("sp", lambda eng: eng.dma_start(out=c_efac[:], in_=efacbc),
                writes=("c_efac",), tag="efac")

        bank_i = [0]
        def rot():
            b = bank_i[0] % 6
            bank_i[0] += 1
            return b

        def D(key, emit, reads=(), writes=(), tag=""):
            sch.add(key, emit, reads, writes, tag)

        def snap5(name, ap, res):
            if name in dbg:
                def e(eng, name=name, ap=ap):
                    return eng.dma_start(out=dbg[name], in_=ap)
                sch.add("gd", e, reads=(res,), writes=(), tag=f"snap_{name}")

        for st5 in range(NT5):
            t0 = st5 * TT5
            for hc in range(16):
                def e(eng, hc=hc, t0=t0):
                    return eng.dma_start(out=xn5[:, hc, :],
                                         in_=xn_s[hc][:, t0:t0 + TT5])
                D("sp", e, reads=("xn_s",), writes=("xn5",), tag=f"xnld{st5}")
            for g in range(4):
                def e(eng, g=g, t0=t0):
                    return eng.dma_start(
                        out=ACC[:, g, :],
                        in_=u[t0 + 128 * g:t0 + 128 * (g + 1), :])
                D("sp", e, writes=("ACC",), tag=f"accini{st5}.{g}")

            for g in range(4):
                def e(eng, g=g):
                    last = None
                    for hc in range(16):
                        last = eng.matmul(
                            PS[:, 6 * 512 + 16 * g:6 * 512 + 16 * (g + 1)],
                            xn5[:, hc, 128 * g:128 * (g + 1)], gwT[:, hc, :],
                            start=(hc == 0), stop=(hc == 15))
                    return last
                D("pe", e, reads=("xn5", "gwT"), writes=("psb6",), tag=f"glog{st5}")
            def e(eng):
                return eng.activation(sco[:], PS[:, 6 * 512:6 * 512 + 64],
                                      AF.Sigmoid)
            D("act", e, reads=("psb6",), writes=("sco",), tag=f"gsig{st5}")
            def e(eng):
                return eng.tensor_tensor(
                    wrk[:], sco[:],
                    c_ebias[:].unsqueeze(1).broadcast_to([128, 4, NE]), ALU.add)
            D("dve", e, reads=("sco", "c_ebias"), writes=("wrk",), tag="sfc")
            sfc4 = wrk[:].rearrange("p g (a b) -> p g a b", a=4)
            pairs = [(0, 1), (0, 2), (0, 3), (1, 2), (1, 3), (2, 3)]
            for pi, (i1, i2) in enumerate(pairs):
                def e(eng, i1=i1, i2=i2, pi=pi):
                    dst = gat[:, :, 4:8] if pi % 2 else gat[:, :, 0:4]
                    return eng.tensor_tensor(dst, sfc4[:, :, :, i1],
                                             sfc4[:, :, :, i2], ALU.add)
                if pi < 2:
                    D("dve", e, reads=("wrk",), writes=("gat",), tag=f"gp{pi}")
                else:
                    def e2(eng, i1=i1, i2=i2):
                        return eng.tensor_tensor(gat[:, :, 8:12],
                                                 sfc4[:, :, :, i1],
                                                 sfc4[:, :, :, i2], ALU.add)
                    D("dve", e2, reads=("wrk",), writes=("gat",), tag=f"gpx")
                    def e3(eng):
                        return eng.tensor_max(gat[:, :, 0:4], gat[:, :, 0:4],
                                              gat[:, :, 8:12])
                    D("dve", e3, reads=("gat",), writes=("gat",), tag=f"gmx")
            def e(eng):
                return eng.tensor_max(gat[:, :, 0:4], gat[:, :, 0:4],
                                      gat[:, :, 4:8])
            D("dve", e, reads=("gat",), writes=("gat",), tag="gmx2")
            def e(eng):
                return eng.tensor_reduce(gat[:, :, 12:13], gat[:, :, 0:4],
                                         AX.X, ALU.max)
            D("dve", e, reads=("gat",), writes=("gat",), tag="gm1")
            def e(eng):
                return eng.tensor_tensor(
                    gat[:, :, 16:20], gat[:, :, 0:4],
                    gat[:, :, 12:13].broadcast_to([128, 4, 4]), ALU.is_ge)
            D("dve", e, reads=("gat",), writes=("gat",), tag="gtie")
            def e(eng):
                return eng.scalar_tensor_tensor(gat[:, :, 16:20], gat[:, :, 16:20],
                                                -1e30, gat[:, :, 0:4],
                                                ALU.mult, ALU.add)
            D("dve", e, reads=("gat",), writes=("gat",), tag="grm")
            def e(eng):
                return eng.tensor_reduce(gat[:, :, 13:14], gat[:, :, 16:20],
                                         AX.X, ALU.max)
            D("dve", e, reads=("gat",), writes=("gat",), tag="gm2")
            def e(eng):
                return eng.tensor_tensor(
                    gat[:, :, 16:20], gat[:, :, 0:4],
                    gat[:, :, 13:14].broadcast_to([128, 4, 4]), ALU.is_ge)
            D("dve", e, reads=("gat",), writes=("gat",), tag="gmask")
            def e(eng):
                emask = gat[:, :, 16:20].unsqueeze(3).broadcast_to([128, 4, 4, 4])
                return eng.tensor_tensor(
                    wrk[:].rearrange("p g (a b) -> p g a b", a=4),
                    wrk[:].rearrange("p g (a b) -> p g a b", a=4),
                    emask, ALU.mult)
            D("dve", e, reads=("wrk", "gat"), writes=("wrk",), tag="emask")
            def e(eng):
                return eng.tensor_scalar_add(wrk[:], wrk[:], 2.0)
            D("dve", e, reads=("wrk",), writes=("wrk",), tag="w2")
            def e(eng):
                return eng.tensor_tensor(
                    wrk[:], wrk[:],
                    c_efac[:].unsqueeze(1).broadcast_to([128, 4, NE]), ALU.mult)
            D("dve", e, reads=("wrk", "c_efac"), writes=("wrk",), tag="wfac")
            for g in range(4):
                def e(eng, g=g):
                    return eng.max(out=m8g[:, g, :], in_=wrk[:, g, :])
                D("dve", e, reads=("wrk",), writes=("m8g",), tag=f"m8.{g}")
            def e(eng):
                return eng.tensor_tensor(
                    sel[:], wrk[:],
                    m8g[:, :, 5:6].broadcast_to([128, 4, NE]), ALU.is_ge)
            D("dve", e, reads=("wrk", "m8g"), writes=("sel",), tag="sel")
            def e(eng):
                return eng.tensor_tensor(sel[:], sel[:], sco[:], ALU.mult)
            D("dve", e, reads=("sel", "sco"), writes=("sel",), tag="wsel")
            def e(eng):
                return eng.tensor_reduce(gat[:, :, 14:15], sel[:], AX.X, ALU.add)
            D("dve", e, reads=("sel",), writes=("gat",), tag="gden")
            def e(eng):
                return eng.tensor_scalar_add(gat[:, :, 14:15], gat[:, :, 14:15],
                                             1e-20)
            D("dve", e, reads=("gat",), writes=("gat",), tag="gde2")
            def e(eng):
                return eng.reciprocal(gat[:, :, 15:16], gat[:, :, 14:15])
            D("dve", e, reads=("gat",), writes=("gat",), tag="grec")
            def e(eng):
                return eng.tensor_tensor(
                    comb[:], sel[:],
                    gat[:, :, 15:16].broadcast_to([128, 4, NE]), ALU.mult)
            D("dve", e, reads=("sel", "gat"), writes=("comb",), tag="comb1")
            def e(eng):
                return eng.tensor_scalar_mul(comb[:], comb[:], 2.5)
            D("dve", e, reads=("comb",), writes=("comb",), tag="comb2")
            if st5 == 0:
                snap5("d_sco", sco[:], "sco")
                snap5("d_comb", comb[:], "comb")

            for chc in range(32):
                def e(eng, chc=chc, st5=st5):
                    return eng.dma_start(
                        out=tfm_v[:, chc, :].rearrange("p (a b) -> p a b", a=4),
                        in_=tfm_s.rearrange("c p t -> p c t")
                        [:, st5 * 4:st5 * 4 + 4, 128 * chc:128 * (chc + 1)])
                D("sp", e, reads=("tfm_s",), writes=("AR1",), tag=f"tfmld{st5}")
            for cb8 in range(4):
                def e(eng, cb8=cb8):
                    return eng.dma_start(
                        out=wo_v[:],
                        in_=wout_b[1024 * cb8:1024 * (cb8 + 1), :]
                        .rearrange("(a p) n -> p a n", p=128))
                D("sp", e, reads=("wout_b",), writes=("AR2",), tag=f"wold{st5}.{cb8}")
                for g in range(4):
                    for nb in range(4):
                        bk = rot()
                        def e(eng, cb8=cb8, g=g, nb=nb, bk=bk):
                            last = None
                            for k in range(8):
                                chc = 8 * cb8 + k
                                last = eng.matmul(
                                    PSB(bk),
                                    tfm_v[:, chc, 128 * g:128 * (g + 1)],
                                    wo_v[:, k, 512 * nb:512 * (nb + 1)],
                                    start=(k == 0), stop=(k == 7))
                            return last
                        D("pe", e, reads=("AR1", "AR2"), writes=(f"psb{bk}",),
                          tag=f"womm{st5}.{cb8}.{g}.{nb}")
                        def e(eng, g=g, nb=nb, bk=bk):
                            return eng.tensor_add(
                                ACC[:, g, 512 * nb:512 * (nb + 1)],
                                ACC[:, g, 512 * nb:512 * (nb + 1)], PSB(bk))
                        D("dve", e, reads=("ACC", f"psb{bk}"), writes=("ACC",),
                          tag=f"woacc{st5}.{cb8}.{g}.{nb}")

            if st5 == 0:
                snap5("d_accw", ACC[:], "ACC")
                snap5("d_tfmv", tfm_v, "AR1")
            for ex in range(18):
                if ex < NE:
                    gsrc = gpT_b[ex].rearrange("(hc p) i -> p hc i", p=128)
                    usrc = upT_b[ex].rearrange("(hc p) i -> p hc i", p=128)
                    dsrc = dpT_b[ex].rearrange("(ic p) n -> p ic n", p=128)
                else:
                    half = ex - NE
                    gsrc = shgT_b[:, IE * half:IE * (half + 1)] \
                        .rearrange("(hc p) i -> p hc i", p=128)
                    usrc = shuT_b[:, IE * half:IE * (half + 1)] \
                        .rearrange("(hc p) i -> p hc i", p=128)
                    dsrc = shdT_b[IE * half:IE * (half + 1), :] \
                        .rearrange("(ic p) n -> p ic n", p=128)
                for hc in range(16):
                    def e(eng, hc=hc, gsrc=gsrc):
                        return eng.dma_start(out=AR1[:, hc, :], in_=gsrc[:, hc, :])
                    D("sp", e, reads=("gpT_b", "shgT_b"), writes=("AR1",),
                      tag=f"gld{st5}.{ex}")
                    def e(eng, hc=hc, usrc=usrc):
                        return eng.dma_start(out=AR2[:, hc, :], in_=usrc[:, hc, :])
                    D("sp", e, reads=("upT_b", "shuT_b"), writes=("AR2",),
                      tag=f"uld{st5}.{ex}")
                for ic in range(11):
                    def e(eng, ic=ic, dsrc=dsrc):
                        return eng.dma_start(out=dpT_v[:, ic, :],
                                             in_=dsrc[:, ic, :])
                    D("sp", e, reads=("dpT_b", "shdT_b"), writes=("BR",),
                      tag=f"dld{st5}.{ex}")
                for ic in range(11):
                    bka = rot()
                    bkb = rot()
                    def e(eng, ic=ic, bka=bka):
                        last = None
                        for hc in range(16):
                            last = eng.matmul(
                                PSB(bka), AR1[:, hc, 128 * ic:128 * (ic + 1)],
                                xn5[:, hc, :], start=(hc == 0), stop=(hc == 15))
                        return last
                    D("pe", e, reads=("AR1", "xn5"), writes=(f"psb{bka}",),
                      tag=f"gmm{st5}.{ex}.{ic}")
                    def e(eng, ic=ic, bkb=bkb):
                        last = None
                        for hc in range(16):
                            last = eng.matmul(
                                PSB(bkb), AR2[:, hc, 128 * ic:128 * (ic + 1)],
                                xn5[:, hc, :], start=(hc == 0), stop=(hc == 15))
                        return last
                    D("pe", e, reads=("AR2", "xn5"), writes=(f"psb{bkb}",),
                      tag=f"umm{st5}.{ex}.{ic}")
                    def e(eng, bka=bka):
                        return eng.activation(sgt[:], PSB(bka), AF.Silu)
                    D("act", e, reads=(f"psb{bka}",), writes=("sgt",),
                      tag=f"sil{st5}.{ex}.{ic}")
                    def e(eng, ic=ic, bkb=bkb):
                        return eng.tensor_tensor(inter[:, ic, :], sgt[:],
                                                 PSB(bkb), ALU.mult)
                    D("dve", e, reads=("sgt", f"psb{bkb}"), writes=("inter",),
                      tag=f"int{st5}.{ex}.{ic}")
                for g in range(4):
                    for nb in range(4):
                        bk = rot()
                        def e(eng, g=g, nb=nb, bk=bk):
                            last = None
                            for ic in range(11):
                                last = eng.matmul(
                                    PSB(bk),
                                    inter[:, ic, 128 * g:128 * (g + 1)],
                                    dpT_v[:, ic, 512 * nb:512 * (nb + 1)],
                                    start=(ic == 0), stop=(ic == 10))
                            return last
                        D("pe", e, reads=("inter", "BR"), writes=(f"psb{bk}",),
                          tag=f"dmm{st5}.{ex}.{g}.{nb}")
                        if ex < NE:
                            def e(eng, g=g, nb=nb, bk=bk, ex=ex):
                                return eng.scalar_tensor_tensor(
                                    ACC[:, g, 512 * nb:512 * (nb + 1)],
                                    PSB(bk), comb[:, g, ex:ex + 1],
                                    ACC[:, g, 512 * nb:512 * (nb + 1)],
                                    ALU.mult, ALU.add)
                            D("dve", e, reads=("ACC", "comb", f"psb{bk}"),
                              writes=("ACC",), tag=f"dacc{st5}.{ex}.{g}.{nb}")
                        else:
                            def e(eng, g=g, nb=nb, bk=bk):
                                return eng.tensor_add(
                                    ACC[:, g, 512 * nb:512 * (nb + 1)],
                                    ACC[:, g, 512 * nb:512 * (nb + 1)], PSB(bk))
                            D("dve", e, reads=("ACC", f"psb{bk}"),
                              writes=("ACC",), tag=f"sacc{st5}.{ex}.{g}.{nb}")

            if st5 == 0:
                snap5("d_acc0", ACC[:], "ACC")
            for g in range(4):
                def e(eng, g=g, t0=t0):
                    return eng.dma_start(
                        out=out[t0 + 128 * g:t0 + 128 * (g + 1), :],
                        in_=ACC[:, g, :])
                D("gd", e, reads=("ACC",), writes=("outd",), tag=f"ost{st5}.{g}")

    if dbg:
        sch.barrier()
        with ExitStack() as des:
            dstg = des.enter_context(nc.sbuf_tensor("dbg_t", [128, 4096], BF16))
            def dump(dst_ap, src_ap, n_rows_of_128, tagn):
                for i in range(n_rows_of_128):
                    def e(eng, i=i):
                        return eng.dma_start(out=dstg[:, 0:src_ap.shape[-1]],
                                             in_=src_ap[i] if src_ap.ndim == 3
                                             else src_ap[128 * i:128 * (i + 1), :])
                    sch.add("sp", e, reads=(), writes=("dbgt",), tag=tagn)
                    def e(eng, i=i):
                        return eng.dma_start(out=dst_ap[i] if dst_ap.ndim == 3
                                             else dst_ap[128 * i:128 * (i + 1), :],
                                             in_=dstg[:, 0:src_ap.shape[-1]])
                    sch.add("gd", e, reads=("dbgt",), writes=("dbgt",), tag=tagn)
            if "d_xn" in dbg:
                dump(dbg["d_xn"], xn_s, 16, "dxn")
            if "d_xbc" in dbg:
                dump(dbg["d_xbc"], xbc_s, 34, "dxbc")
            if "d_siluz" in dbg:
                dump(dbg["d_siluz"], siluz_s, 32, "dsz")
            if "d_tfm" in dbg:
                dump(dbg["d_tfm"], tfm_s, 32, "dtfm")
            if "d_dt" in dbg:
                dstg2 = des.enter_context(nc.sbuf_tensor("dbg_t2", [128, 2048], F32))
                def e(eng):
                    return eng.tensor_copy(dstg2[:], dt_tm[:])
                sch.add("dve", e, reads=("dttm",), writes=("dbg2",), tag="ddt")
                def e(eng):
                    return eng.dma_start(out=dbg["d_dt"], in_=dstg2[:])
                sch.add("gd", e, reads=("dbg2",), writes=("dbg2",), tag="ddt")

    sch.analyze()
    sems = {}
    for k in ("pe", "act", "dve", "pool", "sp", "gd", "ad"):
        sems[k] = es.enter_context(nc.semaphore(f"sem_{k}"))
    with nc.Block() as block:
        sch.emit(nc, block, sems)
    es.close()
    return nc


def _host_prep(inputs):
    f32 = np.float32
    gp = np.ascontiguousarray(inputs["L0_ffn_gate_proj"].transpose(0, 2, 1))
    up = np.ascontiguousarray(inputs["L0_ffn_up_proj"].transpose(0, 2, 1))
    dp = np.ascontiguousarray(inputs["L0_ffn_down_proj"].transpose(0, 2, 1))
    shg = np.ascontiguousarray(inputs["L0_ffn_shared_gate_w"].T)
    shu = np.ascontiguousarray(inputs["L0_ffn_shared_up_w"].T)
    shd = np.ascontiguousarray(inputs["L0_ffn_shared_down_w"].T)
    gwT = np.ascontiguousarray(inputs["L0_ffn_gate_weight"].T)
    cw = np.ascontiguousarray(
        inputs["L0_attn_conv_w"].reshape(34, 128, 4).transpose(1, 0, 2)
        .reshape(128, 136))
    cbv = np.ascontiguousarray(inputs["L0_attn_conv_b"].reshape(34, 128).T)
    rep = lambda v: np.ascontiguousarray(np.broadcast_to(v[None, :], (128, len(v))))
    common = dict(
        winf=np.asarray(inputs["L0_attn_W_in"], f32),
        woutf=np.asarray(inputs["L0_attn_W_out"], f32),
        gpTf=gp, upTf=up, dpTf=dp, shgTf=shg, shuTf=shu, shdTf=shd, gwTf=gwT,
        convw=cw, convb=cbv,
        dtbbc=rep(np.asarray(inputs["L0_attn_dt_bias"], f32)),
        alogbc=rep(np.asarray(inputs["L0_attn_A_log"], f32)),
        dbc=rep(np.asarray(inputs["L0_attn_D"], f32)),
        normwbc=rep(np.asarray(inputs["L0_attn_norm_w"], f32)),
        ebiasbc=rep(np.asarray(inputs["L0_ffn_e_score_correction_bias"], f32)),
        efacbc=rep((1.0 - np.arange(NE, dtype=f32) * 2.0 ** -21)),
        onessq=np.ones((128, 128), f32),
        tri=np.triu(np.ones((128, 128), f32)),
        idb=np.eye(128, dtype=ml_dtypes.bfloat16),
    )
    uf = np.asarray(inputs["L0_attn_u"], f32)
    return [dict(common, u=np.ascontiguousarray(uf[b])) for b in range(B)]


def kernel(**inputs):
    in_maps = _host_prep(inputs)
    nc = build()
    res = run_bass_kernel_spmd(nc, in_maps, core_ids=list(range(B)))
    return np.stack([np.asarray(r["out"], np.float32) for r in res.results], 0)
